# version 11
# baseline (speedup 1.0000x reference)
import contextlib
import numpy as np
import concourse.bass as bass
import concourse.mybir as mybir
from concourse.bass_utils import run_bass_kernel_spmd

DR = mybir.MatmulPerfMode.DoubleRow

F32 = mybir.dt.float32
BF16 = mybir.dt.bfloat16
I32 = mybir.dt.int32
ALU = mybir.AluOpType
ACT = mybir.ActivationFunctionType
AX = mybir.AxisListType

T = 2048
D = 1024
DI = 2048
DS = 128
HF = 4096
PROJ = 4353
NT = T // 128
NCH = 16
L = 128
SUB = 16
NSUB = L // SUB
NPAIR = NSUB * SUB * SUB
EPS = 1e-5

ENGS = ("sync", "vector", "scalar", "tensor", "gpsimd")


DEV_DRAIN = True
SPILLS_OUT = False


def build_core(nc, dev=False, upto="all"):
    dram = {}

    def din(name, shape, dt=F32):
        dram[name] = nc.dram_tensor(name, shape, dt, kind="ExternalInput").ap()
        return dram[name]

    def dout(name, shape, dt=F32):
        dram[name] = nc.dram_tensor(name, shape, dt, kind="ExternalOutput").ap()
        return dram[name]

    def dint(name, shape, dt=F32):
        kind = "ExternalOutput" if (dev or SPILLS_OUT) else "Internal"
        dram[name] = nc.dram_tensor(name, shape, dt, kind=kind).ap()
        return dram[name]

    x_h = din("x", [T, D])
    wpt_h = din("wpt", [D, PROJ])
    convw_h = din("convw", [DI, 4])
    alog_h = din("alog", [DS, 1])
    dp_h = din("dp", [1, DI])
    normw_h = din("normw", [1, DI])
    wot_h = din("wot", [DI, D])
    w1t_h = din("w1t", [D, HF])
    b1_h = din("b1", [1, HF])
    w2t_h = din("w2t", [HF, D])
    b2_h = din("b2", [1, D])
    out_h = dout("out", [T, D])

    z_sp = dint("z_sp", [NT, 16, 128, 128], BF16)
    xm_sp = dint("xm_sp", [NCH, 16, 128, 128], BF16)
    iq_sp = dint("iq_sp", [NT, 8, 128, 128], BF16)
    attn_sp = dint("attn_sp", [T, D], F32)
    gq_sp = dint("gq_sp", [NT, 32, 128, 128], BF16)
    m_sp = dint("m_sp", [NSUB, 16, 16], BF16)
    bc_sp = dint("bc_sp", [2, 128, T], BF16)
    r_sp = dint("r_sp", [16, 128], BF16)
    if dev:
        dint("dbg_xn", [NT, 128, D])
        dint("dbg_sc", [1, 8])
        dint("dbg_tq1", [128, 8 * HF], BF16)
        dint("dbg_tq2", [128, 32 * D], BF16)
        dint("dbg_rows", [4, T])
        dint("dbg_bc", [2, 128, T], BF16)
        dint("dbg_mt", [NCH, 128, 128], BF16)
        dint("dbg_h", [NCH, 128, DI], BF16)
        dint("dbg_y", [NCH, 128, DI], BF16)
        dint("dbg_g", [NT, 128, HF])
        dint("dbg_r1", [9, T], BF16)
        dint("dbg_rr1", [9, HF], BF16)
        dint("dbg_ln", [NT, 128, 8])

    st = contextlib.ExitStack()

    def sb(name, shape, dt=F32):
        return st.enter_context(nc.sbuf_tensor(name, shape, dt))

    ops = {e: [] for e in ENGS}

    with st:
        PA = sb("PA", [128, 32768], BF16)
        PBF = sb("PBF", [128, 32768], BF16)
        PF = sb("PF", [128, 8192], F32)
        PI = sb("PI", [128, 4096], I32)

        def carve(pool, byte_off, cols, dt):
            esz = {F32: 4, I32: 4, BF16: 2}[pool.ap().dtype]
            assert byte_off % esz == 0
            assert dt == pool.ap().dtype
            c0 = byte_off // esz
            return pool[:, c0:c0 + cols]

        wpa = [carve(PA, 0, 1024, BF16), carve(PA, 2048, 1024, BF16)]
        wotb = carve(PA, 0, 16 * D, BF16)
        xmt2b = [carve(PA, 32768, DI, BF16), carve(PA, 36864, DI, BF16)]
        zg2b = [carve(PA, 40960, DI, BF16), carve(PA, 45056, DI, BF16)]
        bT2b = [carve(PA, 49152, 128, BF16), carve(PA, 49408, 128, BF16)]
        tq1 = carve(PA, 0, 8 * HF, BF16)
        tq2 = carve(PA, 0, 32 * D, BF16)
        mask01 = carve(PBF, 0, 256, BF16)
        maskbig = carve(PBF, 512, 256, BF16)
        xs_pad = carve(PBF, 1024, 128, BF16)
        xsT = carve(PBF, 1280, 128, BF16)
        bT_t = carve(PBF, 1536, 128, BF16)
        bstage = carve(PBF, 1792, (NSUB - 1) * SUB, BF16)
        bst_e = carve(PBF, 2048, (NSUB - 1) * SUB, BF16)
        xnT = carve(PBF, 2560, 8 * T, BF16)
        sgnD = carve(PBF, 35328, D, BF16)
        xnbB = carve(PBF, 37376, D, BF16)
        tstB2 = carve(PBF, 39424, D, BF16)
        xmtile = carve(PBF, 35328, T + 4, BF16)
        xw = carve(PBF, 39936, 4 * (T + 4), BF16)
        xm2b = carve(PBF, 56576, T, BF16)
        tstage = carve(PBF, 60672, T, BF16)
        bbar_b = carve(PBF, 2560, T, BF16)
        btil_b = carve(PBF, 6656, T, BF16)
        ctil_b = carve(PBF, 10752, T, BF16)
        cloc_b = carve(PBF, 14848, T, BF16)
        b_fb = carve(PBF, 18944, T, BF16)
        c_fb = carve(PBF, 23040, T, BF16)
        dp_bcb = carve(PBF, 27136, DI, BF16)
        xmt = carve(PBF, 31232, DI, BF16)
        zg_sl = carve(PBF, 35328, DI, BF16)
        ygfm = carve(PBF, 39424, DI, BF16)
        y1b = carve(PBF, 43520, DI, BF16)
        y_sbb = carve(PBF, 47616, DI, BF16)
        h_st = carve(PBF, 51712, DI, BF16)
        epair = carve(PBF, 55808, NPAIR, BF16)
        zpair = carve(PBF, 59904, NPAIR, BF16)
        s1t = carve(PBF, 2560, HF, BF16)
        s2t = carve(PBF, 10752, HF, BF16)
        rr1 = carve(PBF, 51712, HF, BF16)
        rank1 = carve(PBF, 59904, T, BF16)
        sgn = carve(PBF, 2560, HF, BF16)
        gq_tm = carve(PBF, 10752, HF, BF16)
        iq_t = [carve(PBF, 18944, 8 * 128, BF16), carve(PBF, 20992, 8 * 128, BF16)]
        gq_fm = carve(PBF, 33280, 32 * 128, BF16)
        gq2b = [carve(PBF, 33280, 32 * 128, BF16), carve(PBF, 41472, 32 * 128, BF16)]
        at2b = [carve(PF, 0, D, F32), carve(PF, 4096, D, F32)]
        xt2b = [carve(PF, 20480, D, F32), carve(PF, 24576, D, F32)]
        hsrow = carve(PBF, 29184, T, BF16)
        xsrow = carve(PBF, 29184, T, BF16)
        rr2 = carve(PBF, 18944, D, BF16)
        rank2 = carve(PBF, 25088, T, BF16)
        wbuf = [carve(PF, 0, HF, F32), carve(PF, 16384, HF, F32)]
        fprD = carve(PF, 0, D, F32)
        x_tB = carve(PF, 4096, D, F32)
        wpstg = [carve(PF, 0, 1024, F32), carve(PF, 4096, 1024, F32)]
        bc128 = carve(PF, 0, T, F32)
        ysq = carve(PF, 16384, DI, F32)
        xsq = carve(PF, 24576, D, F32)
        erow = carve(PF, 8192, T, F32)
        attn_ev = carve(PF, 8192, D, F32)
        fpr = carve(PF, 0, HF, F32)
        g_t = carve(PF, 16384, HF, F32)
        xa_t = carve(PF, 12288, D, F32)
        out_t = carve(PF, 16384, D, F32)
        xnC = carve(PF, 20480, D, F32)
        p3i = carve(PI, 0, HF, I32)
        p3iD = carve(PI, 0, D, I32)

        s_bc = sb("s_bc", [128, T])
        rowA = sb("rowA", [128, T])
        rowB = sb("rowB", [128, T])
        ident = sb("ident", [128, 128], BF16)
        iot_p = sb("iot_p", [128, 1])
        onesb = sb("onesb", [128, 1], BF16)
        onesf = sb("onesf", [128, 1])
        onesrow = sb("onesrow", [1, 128])
        sx = sb("sx", [128, 1])
        sq2 = sb("sq2", [128, 1])
        mneg = sb("mneg", [128, 1])
        rstd = sb("rstd", [128, 1])
        tv1 = sb("tv1", [128, 1])
        tv2 = sb("tv2", [128, 1])
        hsb16 = sb("hsb16", [128, 1], BF16)
        cst_h = sb("cst_h", [128, 1])
        cst_m5 = sb("cst_m5", [128, 1])
        cst_p5 = sb("cst_p5", [128, 1])
        xs_all = sb("xs_all", [128, NT])
        hs_all = sb("hs_all", [128, NT])
        r_p = sb("r_p", [128, 1])
        c1_p = sb("c1_p", [128, 1])
        c2_p = sb("c2_p", [128, 1])
        asum1 = sb("asum1", [128, 1])
        asum2 = sb("asum2", [128, 1])
        acc_t = sb("acc_t", [128, 1])
        rw1_p = sb("rw1_p", [128, 1])
        rw2_p = sb("rw2_p", [128, 1])
        ws1_p = sb("ws1_p", [128, 1])
        ws2_p = sb("ws2_p", [128, 1])
        wsc = sb("wsc", [1, 8])
        convw_t = sb("convw_t", [128, 64])
        normw_p = sb("normw_p", [128, 16])
        ssum = sb("ssum", [128, 1])
        ac4 = sb("ac4", [128, 4])
        trash = sb("trash", [128, 1])
        trbar_v = sb("trbar_v", [128, 1])
        trbar_s = sb("trbar_s", [128, 1])
        scr_v = sb("scr_v", [128, 512])
        scr_s = sb("scr_s", [128, 512])
        gd_scr = sb("gd_scr", [128, 1])
        a_p = sb("a_p", [128, 1])
        lam_c = sb("lam_c", [128, NCH])
        mt_sb = sb("mt_sb", [128, 128], BF16)

        PSA = st.enter_context(nc.psum_tensor("PSA", [128, HF], F32))

        sems = {}
        for nm in ("b_sync", "b_vector", "b_scalar", "b_tensor", "b_gpsimd",
                   "d_sync", "d_scalar", "d_gpsimd", "d_pf"):
            sems[nm] = st.enter_context(nc.semaphore(nm))
        cnt = {k: 0 for k in sems}

        dirty = {e2: False for e2 in ENGS}
        pend = {e2: {} for e2 in ENGS}
        last_w = {e2: {} for e2 in ENGS}

        real = {"gpsimd": False, "scalar": False}

        def emit(eng, fn):
            p = pend[eng]
            if p:
                items = [(sems[s2], v2) for s2, v2 in p.items()]
                def w(e, _it=items):
                    for sh, sv in _it:
                        e.wait_ge(sh, sv)
                ops[eng].append(w)
                pend[eng] = {}
            dirty[eng] = True
            ops[eng].append(fn)

        def _mk_plain(eng):
            def f(fn):
                if eng in real:
                    real[eng] = True
                emit(eng, lambda e: fn(e))
            return f

        V = _mk_plain("vector")
        S = _mk_plain("scalar")
        P = _mk_plain("tensor")
        G = _mk_plain("gpsimd")

        def _mk_dma(eng, dsem):
            def f(fn):
                cnt[dsem] += 16
                h = sems[dsem]
                emit(eng, lambda e: fn(e).then_inc(h, 16))
            return f

        SY = _mk_dma("sync", "d_sync")
        SD = _mk_dma("scalar", "d_scalar")
        GD = _mk_dma("gpsimd", "d_gpsimd")
        SYp = _mk_dma("sync", "d_pf")
        SDp = _mk_dma("scalar", "d_pf")
        GDp = _mk_dma("gpsimd", "d_pf")

        vd_flag = {"vector": False, "scalar": False}

        def VD():
            vd_flag["vector"] = True

        def SDr():
            vd_flag["scalar"] = True

        def wait_pf():
            for eng in ENGS:
                v = cnt["d_pf"]
                if last_w[eng].get("d_pf", 0) < v:
                    pend[eng]["d_pf"] = max(pend[eng].get("d_pf", 0), v)
                    last_w[eng]["d_pf"] = v

        def bar():
            for eng in ENGS:
                if dirty[eng]:
                    cnt["b_" + eng] += 1
                    h = sems["b_" + eng]
                    if eng == "vector":
                        ops[eng].append(lambda e, h=h: e.drain().then_inc(h, 1))
                        vd_flag["vector"] = False
                    elif eng == "scalar":
                        if real["scalar"] or vd_flag["scalar"]:
                            ops[eng].append(lambda e, h=h: e.drain().then_inc(h, 1))
                            vd_flag["scalar"] = False
                            real["scalar"] = False
                        elif False:
                            ops[eng].append(lambda e, h=h: e.drain().then_inc(h, 1))
                            real["scalar"] = False
                        else:
                            ops[eng].append(lambda e, h=h: e.copy(
                                out=trbar_s[:], in_=iot_p[:]).then_inc(h, 1))
                    elif eng == "tensor":
                        ops[eng].append(lambda e, h=h: e.drain().then_inc(h, 1))
                    elif eng == "gpsimd":
                        if real["gpsimd"]:
                            ops[eng].append(lambda e, h=h: e.drain().then_inc(h, 1))
                            real["gpsimd"] = False
                        else:
                            ops[eng].append(lambda e, h=h: e.nop().then_inc(h, 1))
                    else:
                        ops[eng].append(lambda e, h=h: e.nop().then_inc(h, 1))
                    dirty[eng] = False
            snap = [("b_" + e2, cnt["b_" + e2]) for e2 in ENGS] + \
                   [(d, cnt[d]) for d in ("d_sync", "d_scalar", "d_gpsimd")]
            for eng in ENGS:
                for s2, v2 in snap:
                    if v2 > last_w[eng].get(s2, 0):
                        pend[eng][s2] = max(pend[eng].get(s2, 0), v2)
                        last_w[eng][s2] = v2

        def finalize_waits():
            for eng in ENGS:
                p = pend[eng]
                if p:
                    items = [(sems[s2], v2) for s2, v2 in p.items()]
                    def w(e, _it=items):
                        for sh, sv in _it:
                            e.wait_ge(sh, sv)
                    ops[eng].append(w)
                    pend[eng] = {}

        def xnT_c(k):
            return xnT[:, k * T:(k + 1) * T]

        def tq1_c(k):
            return tq1[:, k * HF:(k + 1) * HF]

        def tq2_c(k):
            return tq2[:, k * D:(k + 1) * D]

        def wot_c(k):
            return wotb[:, k * D:(k + 1) * D]

        def emit_blocks():
            finalize_waits()
            with nc.Block() as block:
                @block.sync
                def _(sync):
                    for f in ops["sync"]:
                        f(sync)

                @block.vector
                def _(vector):
                    for f in ops["vector"]:
                        f(vector)

                @block.scalar
                def _(scalar):
                    for f in ops["scalar"]:
                        f(scalar)

                @block.tensor
                def _(tensor):
                    for f in ops["tensor"]:
                        f(tensor)

                @block.gpsimd
                def _(gpsimd):
                    for f in ops["gpsimd"]:
                        f(gpsimd)

        G(lambda e: e.memset(ident[:], 0.0))
        G(lambda e: e.memset(mt_sb[:], 0.0))
        G(lambda e: e.memset(onesb[:], 1.0))
        G(lambda e: e.memset(onesf[:], 1.0))
        G(lambda e: e.memset(onesrow[:], 1.0))
        G(lambda e: e.memset(xs_pad[:], 0.0))
        G(lambda e: e.memset(cst_h[:], 127.5))
        G(lambda e: e.memset(cst_m5[:], -0.5))
        G(lambda e: e.memset(cst_p5[:], 0.5))
        G(lambda e: e.iota(out=iot_p[:], pattern=[[0, 1]], base=0, channel_multiplier=1,
                           allow_small_or_imprecise_dtypes=True))
        G(lambda e: e.iota(out=rowA[:, 0:128], pattern=[[1, 128]], base=0, channel_multiplier=0,
                           allow_small_or_imprecise_dtypes=True))
        G(lambda e: e.iota(out=rowA[:, 128:384], pattern=[[0, SUB], [1, SUB]], base=0,
                           channel_multiplier=0, allow_small_or_imprecise_dtypes=True))
        G(lambda e: e.iota(out=rowA[:, 384:640], pattern=[[1, SUB], [0, SUB]], base=0,
                           channel_multiplier=0, allow_small_or_imprecise_dtypes=True))
        SY(lambda e: e.dma_start(out=convw_t[:].rearrange("p (b k) -> p b k", k=4),
                                 in_=convw_h.rearrange("(b p) k -> p b k", p=128)))
        SY(lambda e: e.dma_start(out=a_p[:], in_=alog_h))
        SY(lambda e: e.dma_start(out=normw_p[:],
                                 in_=normw_h.rearrange("o (b p) -> o p b", p=128).squeeze(0),
                                 allow_slow_non_contiguous=True))
        bar()
        V(lambda e: e.tensor_tensor(out=mask01[:], in0=rowA[:, 128:384], in1=rowA[:, 384:640],
                                    op=ALU.is_ge))
        bar()
        V(lambda e: e.tensor_scalar(out=maskbig[:], in0=mask01[:], scalar1=-1.0, scalar2=1.0,
                                    op0=ALU.mult, op1=ALU.add))
        bar()
        V(lambda e: e.tensor_scalar(out=maskbig[:], in0=maskbig[:], scalar1=1e30, scalar2=None,
                                    op0=ALU.mult))
        V(lambda e: e.tensor_scalar(out=ident[:], in0=rowA[:, 0:128], scalar1=iot_p[:], scalar2=None,
                                    op0=ALU.is_equal))
        S(lambda e: e.activation(out=a_p[:], in_=a_p[:], func=ACT.Exp))
        bar()
        V(lambda e: e.tensor_scalar(out=a_p[:], in0=a_p[:], scalar1=-1.0, scalar2=None, op0=ALU.mult))
        bar()

        if upto == "p0":
            bar()
            emit_blocks()
            return nc
        G(lambda e: e.memset(asum1[:], 0.0))
        G(lambda e: e.memset(asum2[:], 0.0))
        bar()
        seq = [(w1t_h, k, HF, asum1) for k in range(8)] + \
              [(w2t_h, k, D, asum2) for k in range(32)]
        PFQ = (GDp, SYp, SDp)

        def p1_load(i):
            wh, k, wid, accv = seq[i]
            PFQ[i % 3](lambda e, wh=wh, k=k, wid=wid, i=i: e.dma_start(
                out=wbuf[i % 2][:, 0:wid], in_=wh[k * 128:(k + 1) * 128, :]))

        p1_load(0)
        for i in range(len(seq)):
            wh, k, wid, accv = seq[i]
            wait_pf()
            if i + 1 < len(seq):
                p1_load(i + 1)
            V(lambda e, i=i, wid=wid: e.reduce_sum(out=acc_t[:], in_=wbuf[i % 2][:, 0:wid],
                                                   axis=AX.X, apply_absolute_value=True))
            V(lambda e: e.reduce_sum(out=trash[:], in_=iot_p[:], axis=AX.X))
            V(lambda e: e.tensor_copy(out=trash[:], in_=iot_p[:]))
            V(lambda e, accv=accv: e.tensor_tensor(out=accv[:], in0=accv[:], in1=acc_t[:], op=ALU.add))
            bar()
        P(lambda e: e.matmul(PSA[0:1, 0:1], onesf[:], asum1[:], start=True, stop=True))
        bar()
        S(lambda e: e.copy(out=wsc[0:1, 0:1], in_=PSA[0:1, 0:1]))
        bar()
        P(lambda e: e.matmul(PSA[0:1, 0:1], onesf[:], asum2[:], start=True, stop=True))
        bar()
        S(lambda e: e.copy(out=wsc[0:1, 1:2], in_=PSA[0:1, 0:1]))
        bar()
        V(lambda e: e.tensor_scalar(out=wsc[0:1, 2:4], in0=wsc[0:1, 0:2], scalar1=1.0 / (HF * D),
                                    scalar2=EPS, op0=ALU.mult, op1=ALU.max))
        bar()
        V(lambda e: e.reciprocal(out=wsc[0:1, 0:2], in_=wsc[0:1, 2:4]))
        bar()
        if dev:
            SD(lambda e: e.dma_start(out=dram["dbg_sc"][0:1, 0:8], in_=wsc[0:1, 0:8]))
            SD(lambda e: e.dma_start(out=dram["dbg_rows"][2:3, 0:128].rearrange("o t -> t o"),
                                     in_=asum1[:]))
            SD(lambda e: e.dma_start(out=dram["dbg_rows"][3:4, 0:128].rearrange("o t -> t o"),
                                     in_=asum2[:]))
        for (dst, sl) in ((rw1_p, 0), (rw2_p, 1), (ws1_p, 2), (ws2_p, 3)):
            P(lambda e, sl=sl: e.matmul(PSA[:, 2560:2561], onesrow[:], wsc[0:1, sl:sl + 1],
                                        start=True, stop=True))
            bar()
            S(lambda e, dst=dst: e.copy(out=dst[:], in_=PSA[:, 2560:2561]))
            bar()
        bar()

        if upto == "p1":
            bar()
            emit_blocks()
            return nc
        for tt in range(NT):
            SY(lambda e, tt=tt: e.dma_start(out=x_tB[:], in_=x_h[tt * 128:(tt + 1) * 128, :]))
            bar()
            V(lambda e: e.reduce_sum(out=sx[:], in_=x_tB[:], axis=AX.X))
            V(lambda e: e.reduce_sum(out=trash[:], in_=iot_p[:], axis=AX.X))
            V(lambda e: e.tensor_tensor(out=xsq[:], in0=x_tB[:], in1=x_tB[:], op=ALU.mult))
            V(lambda e: e.tensor_copy(out=trash[:], in_=iot_p[:]))
            V(lambda e: e.reduce_sum(out=sq2[:], in_=xsq[:], axis=AX.X))
            V(lambda e: e.reduce_sum(out=trash[:], in_=iot_p[:], axis=AX.X))
            V(lambda e: e.tensor_copy(out=trash[:], in_=iot_p[:]))
            bar()
            V(lambda e: e.tensor_scalar(out=mneg[:], in0=sx[:], scalar1=-1.0 / D, scalar2=None,
                                        op0=ALU.mult))
            V(lambda e: e.tensor_scalar(out=tv2[:], in0=sq2[:], scalar1=1.0 / D, scalar2=None,
                                        op0=ALU.mult))
            V(lambda e: e.tensor_tensor(out=tv1[:], in0=mneg[:], in1=mneg[:], op=ALU.mult))
            V(lambda e: e.tensor_copy(out=trash[:], in_=iot_p[:]))
            V(lambda e: e.tensor_tensor(out=tv2[:], in0=tv2[:], in1=tv1[:], op=ALU.subtract))
            V(lambda e: e.tensor_copy(out=trash[:], in_=iot_p[:]))
            V(lambda e: e.tensor_scalar(out=tv2[:], in0=tv2[:], scalar1=EPS, scalar2=None, op0=ALU.add))
            V(lambda e: e.tensor_copy(out=trash[:], in_=iot_p[:]))
            V(lambda e: e.reciprocal(out=tv1[:], in_=tv2[:]))
            bar()
            S(lambda e: e.activation(out=rstd[:], in_=tv1[:], func=ACT.Sqrt))
            bar()
            V(lambda e: e.tensor_scalar(out=xnC[:], in0=x_tB[:], scalar1=mneg[:], scalar2=rstd[:],
                                        op0=ALU.add, op1=ALU.mult))
            V(lambda e: e.tensor_copy(out=trash[:], in_=iot_p[:]))
            V(lambda e: e.tensor_copy(out=xnbB[:], in_=xnC[:]))
            V(lambda e: e.reduce_max(out=tv1[:], in_=xnC[:], axis=AX.X, apply_absolute_value=True))
            V(lambda e: e.reduce_sum(out=trash[:], in_=iot_p[:], axis=AX.X))
            V(lambda e: e.tensor_copy(out=trash[:], in_=iot_p[:]))
            V(lambda e: e.tensor_scalar(out=tv1[:], in0=tv1[:], scalar1=EPS, scalar2=None, op0=ALU.max))
            V(lambda e: e.tensor_copy(out=trash[:], in_=iot_p[:]))
            V(lambda e, tt=tt: e.tensor_copy(out=xs_all[:, tt:tt + 1], in_=tv1[:]))
            V(lambda e, tt=tt: e.tensor_copy(out=xs_pad[:, tt:tt + 1], in_=tv1[:]))
            V(lambda e: e.reciprocal(out=tv2[:], in_=tv1[:]))
            V(lambda e: e.tensor_copy(out=trash[:], in_=iot_p[:]))
            V(lambda e: e.tensor_scalar(out=r_p[:], in0=tv2[:], scalar1=127.0, scalar2=None, op0=ALU.mult))
            V(lambda e: e.tensor_copy(out=trash[:], in_=iot_p[:]))
            V(lambda e: e.tensor_scalar(out=p3iD[:], in0=xnC[:], scalar1=r_p[:], scalar2=128.0,
                                        op0=ALU.mult, op1=ALU.add))
            V(lambda e: e.tensor_copy(out=trash[:], in_=iot_p[:]))
            V(lambda e: e.scalar_tensor_tensor(out=fprD[:], in0=xnC[:], scalar=r_p[:],
                                               in1=p3iD[:], op0=ALU.mult, op1=ALU.subtract))
            VD()
            bar()
            if dev:
                SD(lambda e, tt=tt: e.dma_start(out=dram["dbg_xn"][tt], in_=xnC[:]))
            for k in range(8):
                (SY if k % 2 == 0 else SD)(lambda e, k=k, tt=tt: e.dma_start_transpose(
                    out=xnT_c(k)[:, tt * 128:(tt + 1) * 128], in_=xnbB[:, k * 128:(k + 1) * 128]))
            S(lambda e: e.activation(out=sgnD[:], in_=fprD[:], func=ACT.Sign, bias=cst_h[:]))
            bar()
            V(lambda e: e.scalar_tensor_tensor(out=xnbB[:], in0=sgnD[:], scalar=0.0,
                                               in1=p3iD[:], op0=ALU.is_gt, op1=ALU.add))
            VD()
            bar()
            for k in range(8):
                (SY if k % 2 == 0 else SD)(lambda e, k=k: e.dma_start_transpose(
                    out=tstB2[:, k * 128:(k + 1) * 128], in_=xnbB[:, k * 128:(k + 1) * 128]))
            bar()
            SD(lambda e, tt=tt: e.dma_start(out=iq_sp[tt].rearrange("k d t -> d k t"),
                                            in_=tstB2[:].rearrange("d (k t) -> d k t", t=128)))
        bar()

        if upto == "ln":
            bar()
            emit_blocks()
            return nc

        OF_LIST = [(32, 128), (33, 128), (34, 1)] + [(16 + d, 128) for d in range(16)] \
                  + [(z, 128) for z in range(16)]

        def load_of(idx):
            of, M = OF_LIST[idx]
            par = idx % 2
            SY(lambda e, of=of, M=M, par=par: e.dma_start(
                out=wpstg[par][:, 0:8 * M].rearrange("p (k c) -> p k c", c=M),
                in_=wpt_h[:, of * 128:of * 128 + M].rearrange("(k p) c -> p k c", p=128),
                allow_slow_non_contiguous=(M < 128)))

        def proj_mms(idx):
            of, M = OF_LIST[idx]
            par = idx % 2
            V(lambda e, M=M, par=par: e.tensor_copy(out=wpa[par][:, 0:8 * M],
                                                    in_=wpstg[par][:, 0:8 * M]))
            bar()
            if idx + 1 < len(OF_LIST):
                load_of(idx + 1)
            for k in range(8):
                for tc in range(4):
                    P(lambda e, k=k, tc=tc, M=M, par=par: e.matmul(
                        PSA[0:M, tc * 512:(tc + 1) * 512],
                        wpa[par][:, k * M:(k + 1) * M],
                        xnT_c(k)[:, tc * 512:(tc + 1) * 512],
                        start=(k == 0), stop=(k == 7)))
            bar()

        load_of(0)
        bar()
        for idx in range(len(OF_LIST)):
            of, M = OF_LIST[idx]
            proj_mms(idx)
            if of == 32 or of == 33:
                S(lambda e: e.copy(out=tstage[:], in_=PSA[:, 0:T]))
                SDr()
                bar()
                SD(lambda e, i_sp=of - 32: e.dma_start(out=bc_sp[i_sp], in_=tstage[:]))
                bar()
            elif of == 34:
                S(lambda e: e.copy(out=rowA[0:1, :], in_=PSA[0:1, 0:T]))
                bar()
                G(lambda e: e.memset(PBF[:, 17664:28288], 0.0))
                bar()
            elif of >= 16:
                dblk = of - 16
                S(lambda e: e.activation(out=xmtile[:, 4:4 + T], in_=PSA[:, 0:T], func=ACT.Silu))
                bar()
                V(lambda e: e.tensor_scalar(out=xmtile[:, 4:4 + T], in0=xmtile[:, 4:4 + T],
                                            scalar1=8.0, scalar2=None, op0=ALU.min))
                bar()
                for kk in range(4):
                    V(lambda e, kk=kk, dblk=dblk: e.tensor_scalar(
                        out=xw[:, kk * (T + 4) + 4 + (3 - kk): kk * (T + 4) + 4 + (3 - kk) + T],
                        in0=xmtile[:, 4:4 + T],
                        scalar1=convw_t[:, dblk * 4 + kk: dblk * 4 + kk + 1],
                        scalar2=None, op0=ALU.mult))
                bar()
                for kk in range(4):
                    for tc in range(4):
                        base = kk * (T + 4) + 4
                        P(lambda e, kk=kk, tc=tc, base=base: e.matmul(
                            PSA[:, T + tc * 512: T + (tc + 1) * 512], ident[:],
                            xw[:, base + tc * 512: base + (tc + 1) * 512],
                            start=(kk == 0), stop=(kk == 3)))
                bar()
                S(lambda e: e.activation(out=xm2b[:], in_=PSA[:, T:2 * T], func=ACT.Silu))
                bar()
                V(lambda e: e.tensor_scalar(out=xm2b[:], in0=xm2b[:], scalar1=8.0, scalar2=None,
                                            op0=ALU.min))
                VD()
                bar()
                for cblk in range(16):
                    (SY if cblk % 2 == 0 else SD)(lambda e, cblk=cblk: e.dma_start_transpose(
                        out=tstage[:, cblk * 128:(cblk + 1) * 128],
                        in_=xm2b[:, cblk * 128:(cblk + 1) * 128]))
                bar()
                SD(lambda e, dblk=dblk: e.dma_start(
                    out=xm_sp[:, dblk].rearrange("c t d -> t c d"),
                    in_=tstage[:].rearrange("t (c d) -> t c d", d=128)))
            else:
                zblk = of
                S(lambda e: e.activation(out=xm2b[:], in_=PSA[:, 0:T], func=ACT.Silu))
                bar()
                V(lambda e: e.tensor_scalar(out=xm2b[:], in0=xm2b[:], scalar1=8.0, scalar2=None,
                                            op0=ALU.min))
                VD()
                bar()
                SD(lambda e, zblk=zblk: e.dma_start(
                    out=z_sp[:, zblk].rearrange("c z t -> z c t"),
                    in_=xm2b[:].rearrange("z (c t) -> z c t", t=128)))
        bar()

        if upto == "inproj":
            bar()
            emit_blocks()
            return nc

        for k in range(16):
            GD(lambda e, k=k: e.dma_start(out=bc128[:, 0:D], in_=wot_h[k * 128:(k + 1) * 128, :]))
            bar()
            V(lambda e, k=k: e.tensor_copy(out=wot_c(k), in_=bc128[:, 0:D]))
            bar()
        for k in range(16):
            V(lambda e, k=k: e.tensor_scalar(out=wot_c(k), in0=wot_c(k), scalar1=normw_p[:, k:k + 1],
                                             scalar2=None, op0=ALU.mult))
        bar()
        SY(lambda e: e.dma_start(out=bc128[:, 0:DI], in_=dp_h[0:1, :].squeeze(0).partition_broadcast(128)))
        bar()
        V(lambda e: e.tensor_copy(out=dp_bcb[:], in_=bc128[:, 0:DI]))
        bar()
        for j in range(4):
            P(lambda e, j=j: e.matmul(PSA[:, j * 512:(j + 1) * 512], onesrow[:],
                                      rowA[0:1, j * 512:(j + 1) * 512], start=True, stop=True))
        bar()
        S(lambda e: e.copy(out=rowB[:], in_=PSA[:, 0:T]))
        bar()
        S(lambda e: e.activation(out=rowB[:], in_=rowB[:], func=ACT.Exp))
        bar()
        S(lambda e: e.activation(out=rowB[:], in_=rowB[:], func=ACT.Ln, bias=1.0))
        G(lambda e: e.memset(rowA[:], 0.0))
        bar()
        V(lambda e: e.tensor_tensor_scan(out=s_bc[:], data0=rowB[:], data1=rowA[:],
                                         initial=0.0, op0=ALU.add, op1=ALU.add))
        bar()
        if dev:
            SD(lambda e: e.dma_start(out=dram["dbg_rows"][0:1], in_=rowB[0:1, :]))
            SD(lambda e: e.dma_start(out=dram["dbg_rows"][1:2], in_=s_bc[0:1, :]))
        GD(lambda e: e.dma_start(out=b_fb[:], in_=bc_sp[0]))
        GD(lambda e: e.dma_start(out=c_fb[:], in_=bc_sp[1]))
        bar()
        V(lambda e: e.tensor_tensor(out=bbar_b[:], in0=b_fb[:], in1=rowB[:], op=ALU.mult))
        bar()
        V(lambda e: e.tensor_copy(out=rowA[:, 0:1], in_=s_bc[:, 127:128]))
        V(lambda e: e.tensor_tensor(out=rowA[:, 1:NCH], in0=s_bc[:, 2 * L - 1::L],
                                    in1=s_bc[:, L - 1:T - L:L], op=ALU.subtract))
        bar()
        S(lambda e: e.activation(out=lam_c[:], in_=rowA[:, 0:NCH], func=ACT.Exp, scale=a_p[:]))
        bar()
        V(lambda e: e.tensor_tensor(
            out=rowA[:].rearrange("p (c t) -> p c t", t=L),
            in0=s_bc[:, L - 1::L].unsqueeze(2).broadcast_to([128, NCH, L]),
            in1=s_bc[:].rearrange("p (c t) -> p c t", t=L), op=ALU.subtract))
        bar()
        S(lambda e: e.activation(out=erow[:], in_=rowA[:], func=ACT.Exp, scale=a_p[:]))
        bar()
        V(lambda e: e.tensor_tensor(out=btil_b[:], in0=bbar_b[:], in1=erow[:], op=ALU.mult))
        bar()
        G(lambda e: e.memset(rowA[:, 0:L], 0.0))
        bar()
        V(lambda e: e.tensor_copy(
            out=rowA[:, L:T].rearrange("p (c t) -> p c t", t=L),
            in_=s_bc[:, L - 1:T - L:L].unsqueeze(2).broadcast_to([128, NCH - 1, L])))
        bar()
        V(lambda e: e.tensor_tensor(out=rowA[:], in0=s_bc[:], in1=rowA[:], op=ALU.subtract))
        bar()
        S(lambda e: e.activation(out=erow[:], in_=rowA[:], func=ACT.Exp, scale=a_p[:]))
        bar()
        V(lambda e: e.tensor_tensor(out=ctil_b[:], in0=c_fb[:], in1=erow[:], op=ALU.mult))
        bar()
        G(lambda e: e.memset(rowA[:, 0:SUB], 0.0))
        bar()
        V(lambda e: e.tensor_copy(
            out=rowA[:, SUB:T].rearrange("p (c t) -> p c t", t=SUB),
            in_=s_bc[:, SUB - 1:T - SUB:SUB].unsqueeze(2).broadcast_to([128, T // SUB - 1, SUB])))
        bar()
        V(lambda e: e.tensor_tensor(out=rowA[:], in0=s_bc[:], in1=rowA[:], op=ALU.subtract))
        bar()
        S(lambda e: e.activation(out=erow[:], in_=rowA[:], func=ACT.Exp, scale=a_p[:]))
        bar()
        V(lambda e: e.tensor_tensor(out=cloc_b[:], in0=c_fb[:], in1=erow[:], op=ALU.mult))
        VD()
        bar()
        if dev:
            SD(lambda e: e.dma_start(out=dram["dbg_bc"][0], in_=btil_b[:]))
            SD(lambda e: e.dma_start(out=dram["dbg_bc"][1], in_=ctil_b[:]))
            bar()

        if upto == "scanprep":
            bar()
            emit_blocks()
            return nc

        G(lambda e: e.memset(h_st[:], 0.0))
        bar()
        for c in range(NCH):
            V(lambda e, c=c: e.tensor_tensor(
                out=rowA[:].rearrange("p (b tp t) -> p b tp t", tp=SUB, t=SUB),
                in0=s_bc[:, c * L:(c + 1) * L].rearrange("p (b t) -> p b t", t=SUB)
                    .unsqueeze(2).broadcast_to([128, NSUB, SUB, SUB]),
                in1=s_bc[:, c * L:(c + 1) * L].rearrange("p (b tp) -> p b tp", tp=SUB)
                    .unsqueeze(3).broadcast_to([128, NSUB, SUB, SUB]),
                op=ALU.subtract))
            V(lambda e: e.tensor_tensor(
                out=rowA[:].rearrange("p (b q) -> p b q", q=SUB * SUB),
                in0=rowA[:].rearrange("p (b q) -> p b q", q=SUB * SUB),
                in1=maskbig[:].unsqueeze(1).broadcast_to([128, NSUB, SUB * SUB]), op=ALU.add))
            bar()
            S(lambda e: e.activation(out=epair[:], in_=rowA[:], func=ACT.Exp, scale=a_p[:]))
            V(lambda e, c=c: e.tensor_tensor(
                out=zpair[:].rearrange("s (b tp t) -> s b tp t", tp=SUB, t=SUB),
                in0=c_fb[:, c * L:(c + 1) * L].rearrange("s (b t) -> s b t", t=SUB)
                    .unsqueeze(2).broadcast_to([128, NSUB, SUB, SUB]),
                in1=bbar_b[:, c * L:(c + 1) * L].rearrange("s (b tp) -> s b tp", tp=SUB)
                    .unsqueeze(3).broadcast_to([128, NSUB, SUB, SUB]),
                op=ALU.mult))
            bar()
            V(lambda e: e.tensor_tensor(out=zpair[:], in0=zpair[:], in1=epair[:], op=ALU.mult))
            bar()
            for j in range(4):
                P(lambda e, j=j: e.matmul(PSA[0:1, j * 512:(j + 1) * 512], onesb[:],
                                          zpair[:, j * 512:(j + 1) * 512], start=True, stop=True))
            bar()
            S(lambda e: e.copy(out=rowB[0:1, :], in_=PSA[0:1, 0:T]))
            bar()
            V(lambda e: e.tensor_tensor(
                out=zpair[0:1, :].rearrange("o (b q) -> o b q", q=SUB * SUB),
                in0=rowB[0:1, :].rearrange("o (b q) -> o b q", q=SUB * SUB),
                in1=mask01[0:1, :].unsqueeze(1).broadcast_to([1, NSUB, SUB * SUB]), op=ALU.mult))
            bar()
            for b in range(NSUB):
                GD(lambda e, b=b: e.dma_start(
                    out=dram["m_sp"][b].rearrange("tp t -> (tp t)").unsqueeze(0),
                    in_=zpair[0:1, b * 256:(b + 1) * 256]))
            bar()
            for b in range(NSUB):
                GD(lambda e, b=b: e.dma_start(
                    out=mt_sb[b * SUB:(b + 1) * SUB, b * SUB:(b + 1) * SUB],
                    in_=dram["m_sp"][b]))
            bar()
            offs = [0, 0, 16, 48, 96, 160, 240, 336]
            for i in range(1, NSUB):
                w_ = SUB * i
                V(lambda e, c=c, i=i, w_=w_, o=offs[i]: e.tensor_scalar(
                    out=bc128[:, o:o + w_], in0=s_bc[:, c * L:c * L + w_],
                    scalar1=-1.0, scalar2=s_bc[:, c * L + i * SUB - 1:c * L + i * SUB],
                    op0=ALU.mult, op1=ALU.add))
            bar()
            S(lambda e: e.activation(out=erow[:, 0:448], in_=bc128[:, 0:448],
                                     func=ACT.Exp, scale=a_p[:]))
            bar()
            for i in range(1, NSUB):
                w_ = SUB * i
                V(lambda e, c=c, i=i, w_=w_, o=offs[i]: e.tensor_tensor(
                    out=epair[:, o:o + w_], in0=bbar_b[:, c * L:c * L + w_],
                    in1=erow[:, o:o + w_], op=ALU.mult))
            bar()
            for i in range(1, NSUB):
                w_ = SUB * i
                P(lambda e, c=c, i=i, w_=w_, o=offs[i]: e.matmul(
                    PSA[0:w_, 2048 + i * SUB:2048 + (i + 1) * SUB], epair[:, o:o + w_],
                    cloc_b[:, c * L + i * SUB:c * L + (i + 1) * SUB], start=True, stop=True))
            bar()
            for i in range(1, NSUB):
                w_ = SUB * i
                V(lambda e, i=i, w_=w_: e.tensor_copy(
                    out=mt_sb[0:w_, i * SUB:(i + 1) * SUB],
                    in_=PSA[0:w_, 2048 + i * SUB:2048 + (i + 1) * SUB]))
            bar()
            if dev:
                SD(lambda e, c=c: e.dma_start(out=dram["dbg_mt"][c], in_=mt_sb[:]))
            par = c % 2
            if c == 0:
                GDp(lambda e: e.dma_start(out=xmt2b[0][:].rearrange("t (b d) -> t b d", d=128),
                                          in_=xm_sp[0].rearrange("b t d -> t b d")))
                SYp(lambda e: e.dma_start_transpose(out=bT2b[0][:], in_=btil_b[:, 0:L]))
                SDp(lambda e: e.dma_start(out=zg2b[0][:].rearrange("z (b t) -> z b t", t=128),
                                          in_=z_sp[0].rearrange("b z t -> z b t")))
            wait_pf()
            xmt = xmt2b[par]
            zg_sl = zg2b[par]
            bT_t = bT2b[par]
            if c + 1 < NCH:
                GDp(lambda e, c=c, par=par: e.dma_start(
                    out=xmt2b[1 - par][:].rearrange("t (b d) -> t b d", d=128),
                    in_=xm_sp[c + 1].rearrange("b t d -> t b d")))
                SYp(lambda e, c=c, par=par: e.dma_start_transpose(
                    out=bT2b[1 - par][:], in_=btil_b[:, (c + 1) * L:(c + 2) * L]))
                SDp(lambda e, c=c, par=par: e.dma_start(
                    out=zg2b[1 - par][:].rearrange("z (b t) -> z b t", t=128),
                    in_=z_sp[c + 1].rearrange("b z t -> z b t")))
            bar()
            for tc in range(4):
                if c > 0:
                    P(lambda e, c=c, tc=tc: e.matmul(
                        PSA[:, tc * 512:(tc + 1) * 512], ctil_b[:, c * L:(c + 1) * L],
                        h_st[:, tc * 512:(tc + 1) * 512], start=True, stop=False))
                P(lambda e, c=c, tc=tc, xmt=xmt: e.matmul(
                    PSA[:, tc * 512:(tc + 1) * 512], mt_sb[:],
                    xmt[:, tc * 512:(tc + 1) * 512], start=(c == 0), stop=True))
            bar()
            V(lambda e, xmt=xmt: e.tensor_tensor(out=epair[:], in0=xmt[:], in1=dp_bcb[:], op=ALU.mult))
            V(lambda e: e.tensor_copy(out=trash[:], in_=iot_p[:]))
            V(lambda e: e.tensor_tensor(out=y_sbb[:], in0=epair[:], in1=PSA[:, 0:T], op=ALU.add))
            bar()
            if dev:
                SD(lambda e, c=c: e.dma_start(out=dram["dbg_y"][c], in_=y_sbb[:]))
            for tc in range(4):
                P(lambda e, tc=tc, xmt=xmt, bT_t=bT_t: e.matmul(
                    PSA[:, tc * 512:(tc + 1) * 512], bT_t[:],
                    xmt[:, tc * 512:(tc + 1) * 512], start=True, stop=True))
            bar()
            V(lambda e, c=c: e.scalar_tensor_tensor(out=h_st[:], in0=h_st[:], scalar=lam_c[:, c:c + 1],
                                                    in1=PSA[:, 0:T], op0=ALU.mult, op1=ALU.add))
            bar()
            if dev:
                SD(lambda e, c=c: e.dma_start(out=dram["dbg_h"][c], in_=h_st[:]))
            V(lambda e: e.tensor_tensor(out=ysq[:], in0=y_sbb[:], in1=y_sbb[:], op=ALU.mult))
            V(lambda e: e.tensor_copy(out=trash[:], in_=iot_p[:]))
            V(lambda e: e.reduce_sum(out=ssum[:], in_=ysq[:], axis=AX.X))
            V(lambda e: e.reduce_sum(out=trash[:], in_=iot_p[:], axis=AX.X))
            V(lambda e: e.tensor_copy(out=trash[:], in_=iot_p[:]))
            bar()
            V(lambda e: e.tensor_scalar(out=tv1[:], in0=ssum[:], scalar1=1.0 / DI, scalar2=EPS,
                                        op0=ALU.mult, op1=ALU.add))
            V(lambda e: e.tensor_copy(out=trash[:], in_=iot_p[:]))
            V(lambda e: e.reciprocal(out=tv2[:], in_=tv1[:]))
            bar()
            S(lambda e: e.activation(out=rstd[:], in_=tv2[:], func=ACT.Sqrt))
            bar()
            V(lambda e: e.tensor_scalar(out=y1b[:], in0=y_sbb[:], scalar1=rstd[:], scalar2=None,
                                        op0=ALU.mult))
            VD()
            bar()
            for b in range(16):
                (SY if b % 2 == 0 else SD)(lambda e, b=b: e.dma_start_transpose(
                    out=ygfm[:, b * 128:(b + 1) * 128], in_=y1b[:, b * 128:(b + 1) * 128]))
            bar()
            V(lambda e, zg_sl=zg_sl: e.tensor_tensor(out=ygfm[:], in0=ygfm[:], in1=zg_sl[:], op=ALU.mult))
            bar()
            for b in range(16):
                for oc in range(2):
                    P(lambda e, b=b, oc=oc: e.matmul(
                        PSA[:, 3072 + oc * 512:3072 + (oc + 1) * 512],
                        ygfm[:, b * 128:(b + 1) * 128],
                        wot_c(b)[:, oc * 512:(oc + 1) * 512], start=(b == 0), stop=(b == 15)))
            bar()
            S(lambda e: e.copy(out=attn_ev[:], in_=PSA[:, 3072:4096]))
            SDr()
            bar()
            SD(lambda e, c=c: e.dma_start(out=attn_sp[c * 128:(c + 1) * 128, :], in_=attn_ev[:]))
            bar()

        if upto == "scan":
            bar()
            emit_blocks()
            return nc

        PFQ[0](lambda e: e.dma_start(out=wbuf[0][:], in_=w1t_h[0:128, :]))
        for k in range(8):
            buf = wbuf[k % 2]
            wait_pf()
            if k + 1 < 8:
                PFQ[(k + 1) % 3](lambda e, k=k: e.dma_start(
                    out=wbuf[(k + 1) % 2][:], in_=w1t_h[(k + 1) * 128:(k + 2) * 128, :]))
            S(lambda e, buf=buf: e.activation(out=s1t[:], in_=buf[:], func=ACT.Sign,
                                              scale=rw1_p[:], bias=cst_m5[:]))
            S(lambda e, buf=buf: e.activation(out=s2t[:], in_=buf[:], func=ACT.Sign,
                                              scale=rw1_p[:], bias=cst_p5[:]))
            bar()
            V(lambda e, k=k: e.tensor_tensor(out=tq1_c(k), in0=s1t[:], in1=s2t[:], op=ALU.add))
            bar()
        if dev:
            SD(lambda e: e.dma_start(out=dram["dbg_tq1"], in_=tq1))
        stg1 = carve(PBF, 35328, T, BF16)
        stg2 = carve(PBF, 39424, T, BF16)
        for k in range(8):
            for j in range(8):
                P(lambda e, k=k, j=j: e.matmul(PSA[0:1, j * 512:(j + 1) * 512], onesb[:],
                                               tq1_c(k)[:, j * 512:(j + 1) * 512], start=True, stop=True))
            bar()
            S(lambda e: e.copy(out=rowA[0:1, :], in_=PSA[0:1, 0:T]))
            S(lambda e: e.copy(out=rowB[0:1, :], in_=PSA[0:1, T:2 * T]))
            bar()
            V(lambda e: e.tensor_copy(out=stg1[0:1, :], in_=rowA[0:1, :]))
            V(lambda e: e.tensor_copy(out=stg2[0:1, :], in_=rowB[0:1, :]))
            VD()
            bar()
            GD(lambda e, k=k: e.dma_start(out=rr1[1 + k:2 + k, 0:T], in_=stg1[0:1, :]))
            GD(lambda e, k=k: e.dma_start(out=rr1[1 + k:2 + k, T:HF], in_=stg2[0:1, :]))
            bar()
        SY(lambda e: e.dma_start(out=rowA[0:1, :], in_=b1_h[0:1, 0:T]))
        SY(lambda e: e.dma_start(out=rowB[0:1, :], in_=b1_h[0:1, T:HF]))
        bar()
        V(lambda e: e.tensor_copy(out=rr1[0:1, 0:T], in_=rowA[0:1, :]))
        V(lambda e: e.tensor_copy(out=rr1[0:1, T:HF], in_=rowB[0:1, :]))
        SY(lambda e: e.dma_start_transpose(out=xsT[:], in_=xs_pad[:]))
        bar()
        SD(lambda e: e.dma_start(out=dram["r_sp"], in_=xsT[0:16, :]))
        bar()
        SD(lambda e: e.dma_start(out=xsrow[0:1, :],
                                 in_=dram["r_sp"].rearrange("a t -> (a t)").unsqueeze(0)))
        bar()
        V(lambda e: e.tensor_scalar(out=rowA[0:1, :], in0=xsrow[0:1, :], scalar1=wsc[0:1, 2:3],
                                    scalar2=1.0 / 254.0, op0=ALU.mult, op1=ALU.mult))
        bar()
        V(lambda e: e.reciprocal(out=rowB[0:1, :], in_=rowA[0:1, :]))
        bar()
        G(lambda e: e.memset(rank1[0:9, :], -128.0))
        bar()
        V(lambda e: e.tensor_copy(out=rank1[0:1, :], in_=rowB[0:1, :]))
        bar()
        if dev:
            SD(lambda e: e.dma_start(out=dram["dbg_r1"], in_=rank1[0:9, :]))
            SD(lambda e: e.dma_start(out=dram["dbg_rr1"], in_=rr1[0:9, :]))
            bar()

        GD(lambda e: e.dma_start(out=iq_t[0][:].rearrange("d (k t) -> d k t", t=128),
                                 in_=iq_sp[0].rearrange("k d t -> d k t")))
        bar()
        for tt in range(NT):
            par = tt % 2
            if tt + 1 < NT:
                GD(lambda e, tt=tt, par=par: e.dma_start(
                    out=iq_t[1 - par][:].rearrange("d (k t) -> d k t", t=128),
                    in_=iq_sp[tt + 1].rearrange("k d t -> d k t")))
            for half in range(2):
                for k in range(8):
                    for j in range(4):
                        P(lambda e, half=half, k=k, j=j, par=par: e.matmul(
                            PSA[:, half * 2048 + j * 512: half * 2048 + (j + 1) * 512],
                            iq_t[par][:, k * 128:(k + 1) * 128],
                            tq1_c(k)[:, half * 2048 + j * 512: half * 2048 + (j + 1) * 512],
                            start=(k == 0), stop=False))
                for j in range(4):
                    P(lambda e, tt=tt, half=half, j=j: e.matmul(
                        PSA[:, half * 2048 + j * 512: half * 2048 + (j + 1) * 512],
                        rank1[0:9, tt * 128:(tt + 1) * 128],
                        rr1[0:9, half * 2048 + j * 512: half * 2048 + (j + 1) * 512],
                        start=False, stop=True))
            bar()
            V(lambda e, tt=tt: e.tensor_scalar(out=c1_p[:], in0=xs_all[:, tt:tt + 1],
                                               scalar1=ws1_p[:], scalar2=1.0 / 254.0,
                                               op0=ALU.mult, op1=ALU.mult))
            bar()
            for half in range(2):
                S(lambda e, half=half: e.activation(out=g_t[:, half * 2048:(half + 1) * 2048],
                                                    in_=PSA[:, half * 2048:(half + 1) * 2048],
                                                    func=ACT.Gelu, scale=c1_p[:]))
            bar()
            if dev:
                SD(lambda e, tt=tt: e.dma_start(out=dram["dbg_g"][tt], in_=g_t[:]))
            V(lambda e: e.reduce_max(out=tv1[:], in_=g_t[:], axis=AX.X, apply_absolute_value=True))
            V(lambda e: e.reduce_sum(out=trash[:], in_=iot_p[:], axis=AX.X))
            V(lambda e: e.tensor_copy(out=trash[:], in_=iot_p[:]))
            V(lambda e: e.tensor_scalar(out=tv1[:], in0=tv1[:], scalar1=EPS, scalar2=None, op0=ALU.max))
            V(lambda e: e.tensor_copy(out=trash[:], in_=iot_p[:]))
            V(lambda e, tt=tt: e.tensor_copy(out=hs_all[:, tt:tt + 1], in_=tv1[:]))
            V(lambda e, tt=tt: e.tensor_copy(out=xs_pad[:, 64 + tt:65 + tt], in_=tv1[:]))
            V(lambda e: e.reciprocal(out=tv2[:], in_=tv1[:]))
            V(lambda e: e.tensor_copy(out=trash[:], in_=iot_p[:]))
            V(lambda e: e.tensor_scalar(out=r_p[:], in0=tv2[:], scalar1=127.0, scalar2=None,
                                        op0=ALU.mult))
            V(lambda e: e.tensor_copy(out=trash[:], in_=iot_p[:]))
            V(lambda e: e.tensor_scalar(out=p3i[:], in0=g_t[:], scalar1=r_p[:], scalar2=128.0,
                                        op0=ALU.mult, op1=ALU.add))
            V(lambda e: e.tensor_copy(out=trash[:], in_=iot_p[:]))
            V(lambda e: e.scalar_tensor_tensor(out=fpr[:], in0=g_t[:], scalar=r_p[:],
                                               in1=p3i[:], op0=ALU.mult, op1=ALU.subtract))
            bar()
            S(lambda e: e.activation(out=sgn[:], in_=fpr[:], func=ACT.Sign, bias=cst_h[:]))
            bar()
            V(lambda e: e.scalar_tensor_tensor(out=gq_tm[:], in0=sgn[:], scalar=0.0,
                                               in1=p3i[:], op0=ALU.is_gt, op1=ALU.add))
            VD()
            bar()
            for jb in range(32):
                (SY if jb % 2 == 0 else SD)(lambda e, jb=jb: e.dma_start_transpose(
                    out=gq_fm[:, jb * 128:(jb + 1) * 128], in_=gq_tm[:, jb * 128:(jb + 1) * 128]))
            bar()
            SD(lambda e, tt=tt: e.dma_start(out=gq_sp[tt].rearrange("b j t -> j b t"),
                                            in_=gq_fm[:].rearrange("j (b t) -> j b t", t=128)))
            bar()

        if upto == "mm1":
            bar()
            emit_blocks()
            return nc

        PFQ[0](lambda e: e.dma_start(out=wbuf[0][:, 0:D], in_=w2t_h[0:128, :]))
        for k in range(32):
            buf = wbuf[k % 2]
            wait_pf()
            if k + 1 < 32:
                PFQ[(k + 1) % 3](lambda e, k=k: e.dma_start(
                    out=wbuf[(k + 1) % 2][:, 0:D], in_=w2t_h[(k + 1) * 128:(k + 2) * 128, :]))
            S(lambda e, buf=buf: e.activation(out=s1t[:, 0:D], in_=buf[:, 0:D], func=ACT.Sign,
                                              scale=rw2_p[:], bias=cst_m5[:]))
            S(lambda e, buf=buf: e.activation(out=s2t[:, 0:D], in_=buf[:, 0:D], func=ACT.Sign,
                                              scale=rw2_p[:], bias=cst_p5[:]))
            bar()
            V(lambda e, k=k: e.tensor_tensor(out=tq2_c(k), in0=s1t[:, 0:D], in1=s2t[:, 0:D], op=ALU.add))
            bar()
        if dev:
            SD(lambda e: e.dma_start(out=dram["dbg_tq2"], in_=tq2))
        stg1b = carve(PBF, 35328, 2 * T, BF16)
        for kg in range(8):
            for kk in range(4):
                for j in range(2):
                    P(lambda e, kg=kg, kk=kk, j=j: e.matmul(
                        PSA[0:1, kk * 1024 + j * 512: kk * 1024 + (j + 1) * 512], onesb[:],
                        tq2_c(kg * 4 + kk)[:, j * 512:(j + 1) * 512], start=True, stop=True))
            bar()
            S(lambda e: e.copy(out=rowA[0:1, :], in_=PSA[0:1, 0:T]))
            S(lambda e: e.copy(out=rowB[0:1, :], in_=PSA[0:1, T:2 * T]))
            bar()
            V(lambda e: e.tensor_copy(out=stg1b[0:1, 0:T], in_=rowA[0:1, :]))
            V(lambda e: e.tensor_copy(out=stg1b[0:1, T:2 * T], in_=rowB[0:1, :]))
            VD()
            bar()
            for kk in range(4):
                GD(lambda e, kg=kg, kk=kk: e.dma_start(out=rr2[1 + kg * 4 + kk:2 + kg * 4 + kk, 0:D],
                                                       in_=stg1b[0:1, kk * D:(kk + 1) * D]))
            bar()
        SY(lambda e: e.dma_start(out=rowA[0:1, 0:D], in_=b2_h))
        bar()
        V(lambda e: e.tensor_copy(out=rr2[0:1, 0:D], in_=rowA[0:1, 0:D]))
        SY(lambda e: e.dma_start_transpose(out=xsT[:], in_=xs_pad[:]))
        bar()
        SD(lambda e: e.dma_start(out=dram["r_sp"], in_=xsT[64:80, :]))
        bar()
        SD(lambda e: e.dma_start(out=hsrow[0:1, :],
                                 in_=dram["r_sp"].rearrange("a t -> (a t)").unsqueeze(0)))
        bar()
        V(lambda e: e.tensor_scalar(out=rowA[0:1, :], in0=hsrow[0:1, :], scalar1=wsc[0:1, 3:4],
                                    scalar2=1.0 / 254.0, op0=ALU.mult, op1=ALU.mult))
        bar()
        V(lambda e: e.reciprocal(out=rowB[0:1, :], in_=rowA[0:1, :]))
        bar()
        G(lambda e: e.memset(rank2[0:33, :], -128.0))
        bar()
        V(lambda e: e.tensor_copy(out=rank2[0:1, :], in_=rowB[0:1, :]))
        bar()

        def p9_load(tt):
            GDp(lambda e, tt=tt: e.dma_start(
                out=gq2b[tt % 2][:].rearrange("j (b t) -> j b t", t=128),
                in_=gq_sp[tt].rearrange("b j t -> j b t")))
            SDp(lambda e, tt=tt: e.dma_start(out=at2b[tt % 2][:], in_=attn_sp[tt * 128:(tt + 1) * 128, :]))
            SYp(lambda e, tt=tt: e.dma_start(out=xt2b[tt % 2][:], in_=x_h[tt * 128:(tt + 1) * 128, :]))

        p9_load(0)
        for tt in range(NT):
            gq_l = gq2b[tt % 2]
            at_l = at2b[tt % 2]
            xt_l = xt2b[tt % 2]
            wait_pf()
            if tt + 1 < NT:
                p9_load(tt + 1)
            V(lambda e, xt_l=xt_l, at_l=at_l: e.tensor_tensor(
                out=xa_t[:], in0=xt_l[:], in1=at_l[:], op=ALU.add))
            for k in range(32):
                for j in range(2):
                    P(lambda e, k=k, j=j, gq_l=gq_l: e.matmul(
                        PSA[:, j * 512:(j + 1) * 512], gq_l[:, k * 128:(k + 1) * 128],
                        tq2_c(k)[:, j * 512:(j + 1) * 512], start=(k == 0), stop=False))
            for j in range(2):
                P(lambda e, tt=tt, j=j: e.matmul(
                    PSA[:, j * 512:(j + 1) * 512], rank2[0:33, tt * 128:(tt + 1) * 128],
                    rr2[0:33, j * 512:(j + 1) * 512], start=False, stop=True))
            bar()
            V(lambda e, tt=tt: e.tensor_scalar(out=c2_p[:], in0=hs_all[:, tt:tt + 1],
                                               scalar1=ws2_p[:], scalar2=1.0 / 254.0,
                                               op0=ALU.mult, op1=ALU.mult))
            bar()
            V(lambda e: e.scalar_tensor_tensor(out=out_t[:], in0=PSA[:, 0:D], scalar=c2_p[:],
                                               in1=xa_t[:], op0=ALU.mult, op1=ALU.add))
            VD()
            bar()
            SY(lambda e, tt=tt: e.dma_start(out=out_h[tt * 128:(tt + 1) * 128, :], in_=out_t[:]))
            bar()

        bar()
        emit_blocks()
        return nc


def host_prep(inputs):
    f = np.ascontiguousarray
    wpt = f(inputs["L0_attn_in_proj_w"].T)
    convw = f(inputs["L0_attn_conv_w"].reshape(DI, 4))
    alog = f(inputs["L0_attn_A_log"].reshape(DS, 1))
    dp = f(inputs["L0_attn_Dp"].reshape(1, DI))
    normw = f(inputs["L0_attn_norm_w"].reshape(1, DI))
    wot = f(inputs["L0_attn_out_proj_w"].T)
    w1t = f(inputs["L0_ffn_W1"].T)
    b1 = f(inputs["L0_ffn_b1"].reshape(1, HF))
    w2t = f(inputs["L0_ffn_W2"].T)
    b2 = f(inputs["L0_ffn_b2"].reshape(1, D))
    x = inputs["L0_attn_x"]
    in_maps = []
    for c in range(8):
        in_maps.append({
            "x": f(x[c]), "wpt": wpt, "convw": convw, "alog": alog, "dp": dp,
            "normw": normw, "wot": wot, "w1t": w1t, "b1": b1, "w2t": w2t, "b2": b2,
        })
    return in_maps


def kernel(**inputs):
    in_maps = host_prep(inputs)
    nc = bass.Bass("TRN2", target_bir_lowering=False, debug=False, detect_race_conditions=False)
    build_core(nc, dev=False, upto="all")
    res = run_bass_kernel_spmd(nc, in_maps, core_ids=list(range(8)))
    out = np.stack([np.asarray(r["out"]) for r in res.results], axis=0)
    return out.astype(np.float32)


# version 12 (final)
# speedup vs baseline: 1.0140x; 1.0140x over previous
import contextlib
import numpy as np
import concourse.bass as bass
import concourse.mybir as mybir
from concourse.bass_utils import run_bass_kernel_spmd

DR = mybir.MatmulPerfMode.DoubleRow

F32 = mybir.dt.float32
BF16 = mybir.dt.bfloat16
I32 = mybir.dt.int32
ALU = mybir.AluOpType
ACT = mybir.ActivationFunctionType
AX = mybir.AxisListType

T = 2048
D = 1024
DI = 2048
DS = 128
HF = 4096
PROJ = 4353
NT = T // 128
NCH = 16
L = 128
SUB = 16
NSUB = L // SUB
NPAIR = NSUB * SUB * SUB
EPS = 1e-5

ENGS = ("sync", "vector", "scalar", "tensor", "gpsimd")


DEV_DRAIN = True
SPILLS_OUT = False


def build_core(nc, dev=False, upto="all"):
    dram = {}

    def din(name, shape, dt=F32):
        dram[name] = nc.dram_tensor(name, shape, dt, kind="ExternalInput").ap()
        return dram[name]

    def dout(name, shape, dt=F32):
        dram[name] = nc.dram_tensor(name, shape, dt, kind="ExternalOutput").ap()
        return dram[name]

    def dint(name, shape, dt=F32):
        kind = "ExternalOutput" if (dev or SPILLS_OUT) else "Internal"
        dram[name] = nc.dram_tensor(name, shape, dt, kind=kind).ap()
        return dram[name]

    x_h = din("x", [T, D])
    wpt_h = din("wpt", [D, PROJ])
    convw_h = din("convw", [DI, 4])
    alog_h = din("alog", [DS, 1])
    dp_h = din("dp", [1, DI])
    normw_h = din("normw", [1, DI])
    wot_h = din("wot", [DI, D])
    w1t_h = din("w1t", [D, HF])
    b1_h = din("b1", [1, HF])
    w2t_h = din("w2t", [HF, D])
    b2_h = din("b2", [1, D])
    out_h = dout("out", [T, D])

    z_sp = dint("z_sp", [NT, 16, 128, 128], BF16)
    xm_sp = dint("xm_sp", [NCH, 16, 128, 128], BF16)
    iq_sp = dint("iq_sp", [NT, 8, 128, 128], BF16)
    attn_sp = dint("attn_sp", [T, D], F32)
    gq_sp = dint("gq_sp", [NT, 32, 128, 128], BF16)
    m_sp = dint("m_sp", [NSUB, 16, 16], BF16)
    bc_sp = dint("bc_sp", [2, 128, T], BF16)
    r_sp = dint("r_sp", [16, 128], BF16)
    if dev:
        dint("dbg_xn", [NT, 128, D])
        dint("dbg_sc", [1, 8])
        dint("dbg_tq1", [128, 8 * HF], BF16)
        dint("dbg_tq2", [128, 32 * D], BF16)
        dint("dbg_rows", [4, T])
        dint("dbg_bc", [2, 128, T], BF16)
        dint("dbg_mt", [NCH, 128, 128], BF16)
        dint("dbg_h", [NCH, 128, DI], BF16)
        dint("dbg_y", [NCH, 128, DI], BF16)
        dint("dbg_g", [NT, 128, HF])
        dint("dbg_r1", [9, T], BF16)
        dint("dbg_rr1", [9, HF], BF16)
        dint("dbg_ln", [NT, 128, 8])

    st = contextlib.ExitStack()

    def sb(name, shape, dt=F32):
        return st.enter_context(nc.sbuf_tensor(name, shape, dt))

    ops = {e: [] for e in ENGS}

    with st:
        PA = sb("PA", [128, 32768], BF16)
        PBF = sb("PBF", [128, 32768], BF16)
        PF = sb("PF", [128, 8192], F32)
        PI = sb("PI", [128, 4096], I32)

        def carve(pool, byte_off, cols, dt):
            esz = {F32: 4, I32: 4, BF16: 2}[pool.ap().dtype]
            assert byte_off % esz == 0
            assert dt == pool.ap().dtype
            c0 = byte_off // esz
            return pool[:, c0:c0 + cols]

        wpa = [carve(PA, 0, 1024, BF16), carve(PA, 2048, 1024, BF16)]
        wotb = carve(PA, 0, 16 * D, BF16)
        xmt2b = [carve(PA, 32768, DI, BF16), carve(PA, 36864, DI, BF16)]
        zg2b = [carve(PA, 40960, DI, BF16), carve(PA, 45056, DI, BF16)]
        bT2b = [carve(PA, 49152, 128, BF16), carve(PA, 49408, 128, BF16)]
        tq1 = carve(PA, 0, 8 * HF, BF16)
        tq2 = carve(PA, 0, 32 * D, BF16)
        mask01 = carve(PBF, 0, 256, BF16)
        maskbig = carve(PBF, 512, 256, BF16)
        xs_pad = carve(PBF, 1024, 128, BF16)
        xsT = carve(PBF, 1280, 128, BF16)
        bT_t = carve(PBF, 1536, 128, BF16)
        bstage = carve(PBF, 1792, (NSUB - 1) * SUB, BF16)
        bst_e = carve(PBF, 2048, (NSUB - 1) * SUB, BF16)
        xnT = carve(PBF, 2560, 8 * T, BF16)
        sgnD = carve(PBF, 35328, D, BF16)
        xnbB = carve(PBF, 37376, D, BF16)
        tstB2 = carve(PBF, 39424, D, BF16)
        xmtile = carve(PBF, 35328, T + 4, BF16)
        xw = carve(PBF, 39936, 4 * (T + 4), BF16)
        xm2b = carve(PBF, 56576, T, BF16)
        tstage = carve(PBF, 60672, T, BF16)
        bbar_b = carve(PBF, 2560, T, BF16)
        btil_b = carve(PBF, 6656, T, BF16)
        ctil_b = carve(PBF, 10752, T, BF16)
        cloc_b = carve(PBF, 14848, T, BF16)
        b_fb = carve(PBF, 18944, T, BF16)
        c_fb = carve(PBF, 23040, T, BF16)
        dp_bcb = carve(PBF, 27136, DI, BF16)
        xmt = carve(PBF, 31232, DI, BF16)
        zg_sl = carve(PBF, 35328, DI, BF16)
        ygfm = carve(PBF, 39424, DI, BF16)
        y1b = carve(PBF, 43520, DI, BF16)
        y_sbb = carve(PBF, 47616, DI, BF16)
        h_st = carve(PBF, 51712, DI, BF16)
        epair = carve(PBF, 55808, NPAIR, BF16)
        zpair = carve(PBF, 59904, NPAIR, BF16)
        s1t = carve(PBF, 2560, HF, BF16)
        s2t = carve(PBF, 10752, HF, BF16)
        rr1 = carve(PBF, 51712, HF, BF16)
        rank1 = carve(PBF, 59904, T, BF16)
        sgn = carve(PBF, 2560, HF, BF16)
        gq_tm = carve(PBF, 10752, HF, BF16)
        iq_t = [carve(PBF, 18944, 8 * 128, BF16), carve(PBF, 20992, 8 * 128, BF16)]
        gq_fm = carve(PBF, 33280, 32 * 128, BF16)
        gq2b = [carve(PBF, 33280, 32 * 128, BF16), carve(PBF, 41472, 32 * 128, BF16)]
        at2b = [carve(PF, 0, D, F32), carve(PF, 4096, D, F32)]
        xt2b = [carve(PF, 20480, D, F32), carve(PF, 24576, D, F32)]
        hsrow = carve(PBF, 29184, T, BF16)
        xsrow = carve(PBF, 29184, T, BF16)
        rr2 = carve(PBF, 18944, D, BF16)
        rank2 = carve(PBF, 25088, T, BF16)
        wbuf = [carve(PF, 0, HF, F32), carve(PF, 16384, HF, F32)]
        fprD = carve(PF, 0, D, F32)
        x_tB = carve(PF, 4096, D, F32)
        x2b_p2 = [carve(PF, 4096, D, F32), carve(PF, 8192, D, F32)]
        wpstg = [carve(PF, 0, 1024, F32), carve(PF, 4096, 1024, F32)]
        bc128 = carve(PF, 0, T, F32)
        ysq = carve(PF, 16384, DI, F32)
        xsq = carve(PF, 24576, D, F32)
        erow = carve(PF, 8192, T, F32)
        attn_ev = carve(PF, 8192, D, F32)
        fpr = carve(PF, 0, HF, F32)
        g_t = carve(PF, 16384, HF, F32)
        xa_t = carve(PF, 12288, D, F32)
        out_t = carve(PF, 16384, D, F32)
        xnC = carve(PF, 20480, D, F32)
        p3i = carve(PI, 0, HF, I32)
        p3iD = carve(PI, 0, D, I32)

        s_bc = sb("s_bc", [128, T])
        rowA = sb("rowA", [128, T])
        rowB = sb("rowB", [128, T])
        ident = sb("ident", [128, 128], BF16)
        iot_p = sb("iot_p", [128, 1])
        onesb = sb("onesb", [128, 1], BF16)
        onesf = sb("onesf", [128, 1])
        onesrow = sb("onesrow", [1, 128])
        sx = sb("sx", [128, 1])
        sq2 = sb("sq2", [128, 1])
        mneg = sb("mneg", [128, 1])
        rstd = sb("rstd", [128, 1])
        tv1 = sb("tv1", [128, 1])
        tv2 = sb("tv2", [128, 1])
        hsb16 = sb("hsb16", [128, 1], BF16)
        cst_h = sb("cst_h", [128, 1])
        cst_m5 = sb("cst_m5", [128, 1])
        cst_p5 = sb("cst_p5", [128, 1])
        xs_all = sb("xs_all", [128, NT])
        hs_all = sb("hs_all", [128, NT])
        r_p = sb("r_p", [128, 1])
        c1_p = sb("c1_p", [128, 1])
        c2_p = sb("c2_p", [128, 1])
        asum1 = sb("asum1", [128, 1])
        asum2 = sb("asum2", [128, 1])
        acc_t = sb("acc_t", [128, 1])
        rw1_p = sb("rw1_p", [128, 1])
        rw2_p = sb("rw2_p", [128, 1])
        ws1_p = sb("ws1_p", [128, 1])
        ws2_p = sb("ws2_p", [128, 1])
        wsc = sb("wsc", [1, 8])
        convw_t = sb("convw_t", [128, 64])
        normw_p = sb("normw_p", [128, 16])
        ssum = sb("ssum", [128, 1])
        ac4 = sb("ac4", [128, 4])
        trash = sb("trash", [128, 1])
        trbar_v = sb("trbar_v", [128, 1])
        trbar_s = sb("trbar_s", [128, 1])
        scr_v = sb("scr_v", [128, 512])
        scr_s = sb("scr_s", [128, 512])
        gd_scr = sb("gd_scr", [128, 1])
        a_p = sb("a_p", [128, 1])
        lam_c = sb("lam_c", [128, NCH])
        mt_sb = sb("mt_sb", [128, 128], BF16)

        PSA = st.enter_context(nc.psum_tensor("PSA", [128, HF], F32))

        sems = {}
        for nm in ("b_sync", "b_vector", "b_scalar", "b_tensor", "b_gpsimd",
                   "d_sync", "d_scalar", "d_gpsimd", "d_pf"):
            sems[nm] = st.enter_context(nc.semaphore(nm))
        cnt = {k: 0 for k in sems}

        dirty = {e2: False for e2 in ENGS}
        pend = {e2: {} for e2 in ENGS}
        last_w = {e2: {} for e2 in ENGS}

        real = {"gpsimd": False, "scalar": False}

        def emit(eng, fn):
            p = pend[eng]
            if p:
                items = [(sems[s2], v2) for s2, v2 in p.items()]
                def w(e, _it=items):
                    for sh, sv in _it:
                        e.wait_ge(sh, sv)
                ops[eng].append(w)
                pend[eng] = {}
            dirty[eng] = True
            ops[eng].append(fn)

        def _mk_plain(eng):
            def f(fn):
                if eng in real:
                    real[eng] = True
                emit(eng, lambda e: fn(e))
            return f

        V = _mk_plain("vector")
        S = _mk_plain("scalar")
        P = _mk_plain("tensor")
        G = _mk_plain("gpsimd")

        def _mk_dma(eng, dsem):
            def f(fn):
                cnt[dsem] += 16
                h = sems[dsem]
                emit(eng, lambda e: fn(e).then_inc(h, 16))
            return f

        SY = _mk_dma("sync", "d_sync")
        SD = _mk_dma("scalar", "d_scalar")
        GD = _mk_dma("gpsimd", "d_gpsimd")
        SYp = _mk_dma("sync", "d_pf")
        SDp = _mk_dma("scalar", "d_pf")
        GDp = _mk_dma("gpsimd", "d_pf")

        vd_flag = {"vector": False, "scalar": False}

        def VD():
            vd_flag["vector"] = True

        def SDr():
            vd_flag["scalar"] = True

        def wait_pf():
            for eng in ENGS:
                v = cnt["d_pf"]
                if last_w[eng].get("d_pf", 0) < v:
                    pend[eng]["d_pf"] = max(pend[eng].get("d_pf", 0), v)
                    last_w[eng]["d_pf"] = v

        def bar():
            for eng in ENGS:
                if dirty[eng]:
                    cnt["b_" + eng] += 1
                    h = sems["b_" + eng]
                    if eng == "vector":
                        ops[eng].append(lambda e, h=h: e.drain().then_inc(h, 1))
                        vd_flag["vector"] = False
                    elif eng == "scalar":
                        if real["scalar"] or vd_flag["scalar"]:
                            ops[eng].append(lambda e, h=h: e.drain().then_inc(h, 1))
                            vd_flag["scalar"] = False
                            real["scalar"] = False
                        elif False:
                            ops[eng].append(lambda e, h=h: e.drain().then_inc(h, 1))
                            real["scalar"] = False
                        else:
                            ops[eng].append(lambda e, h=h: e.copy(
                                out=trbar_s[:], in_=iot_p[:]).then_inc(h, 1))
                    elif eng == "tensor":
                        ops[eng].append(lambda e, h=h: e.drain().then_inc(h, 1))
                    elif eng == "gpsimd":
                        if real["gpsimd"]:
                            ops[eng].append(lambda e, h=h: e.drain().then_inc(h, 1))
                            real["gpsimd"] = False
                        else:
                            ops[eng].append(lambda e, h=h: e.nop().then_inc(h, 1))
                    else:
                        ops[eng].append(lambda e, h=h: e.nop().then_inc(h, 1))
                    dirty[eng] = False
            snap = [("b_" + e2, cnt["b_" + e2]) for e2 in ENGS] + \
                   [(d, cnt[d]) for d in ("d_sync", "d_scalar", "d_gpsimd")]
            for eng in ENGS:
                for s2, v2 in snap:
                    if v2 > last_w[eng].get(s2, 0):
                        pend[eng][s2] = max(pend[eng].get(s2, 0), v2)
                        last_w[eng][s2] = v2

        def finalize_waits():
            for eng in ENGS:
                p = pend[eng]
                if p:
                    items = [(sems[s2], v2) for s2, v2 in p.items()]
                    def w(e, _it=items):
                        for sh, sv in _it:
                            e.wait_ge(sh, sv)
                    ops[eng].append(w)
                    pend[eng] = {}

        def xnT_c(k):
            return xnT[:, k * T:(k + 1) * T]

        def tq1_c(k):
            return tq1[:, k * HF:(k + 1) * HF]

        def tq2_c(k):
            return tq2[:, k * D:(k + 1) * D]

        def wot_c(k):
            return wotb[:, k * D:(k + 1) * D]

        def emit_blocks():
            finalize_waits()
            with nc.Block() as block:
                @block.sync
                def _(sync):
                    for f in ops["sync"]:
                        f(sync)

                @block.vector
                def _(vector):
                    for f in ops["vector"]:
                        f(vector)

                @block.scalar
                def _(scalar):
                    for f in ops["scalar"]:
                        f(scalar)

                @block.tensor
                def _(tensor):
                    for f in ops["tensor"]:
                        f(tensor)

                @block.gpsimd
                def _(gpsimd):
                    for f in ops["gpsimd"]:
                        f(gpsimd)

        G(lambda e: e.memset(ident[:], 0.0))
        G(lambda e: e.memset(mt_sb[:], 0.0))
        G(lambda e: e.memset(onesb[:], 1.0))
        G(lambda e: e.memset(onesf[:], 1.0))
        G(lambda e: e.memset(onesrow[:], 1.0))
        G(lambda e: e.memset(xs_pad[:], 0.0))
        G(lambda e: e.memset(cst_h[:], 127.5))
        G(lambda e: e.memset(cst_m5[:], -0.5))
        G(lambda e: e.memset(cst_p5[:], 0.5))
        G(lambda e: e.iota(out=iot_p[:], pattern=[[0, 1]], base=0, channel_multiplier=1,
                           allow_small_or_imprecise_dtypes=True))
        G(lambda e: e.iota(out=rowA[:, 0:128], pattern=[[1, 128]], base=0, channel_multiplier=0,
                           allow_small_or_imprecise_dtypes=True))
        G(lambda e: e.iota(out=rowA[:, 128:384], pattern=[[0, SUB], [1, SUB]], base=0,
                           channel_multiplier=0, allow_small_or_imprecise_dtypes=True))
        G(lambda e: e.iota(out=rowA[:, 384:640], pattern=[[1, SUB], [0, SUB]], base=0,
                           channel_multiplier=0, allow_small_or_imprecise_dtypes=True))
        SY(lambda e: e.dma_start(out=convw_t[:].rearrange("p (b k) -> p b k", k=4),
                                 in_=convw_h.rearrange("(b p) k -> p b k", p=128)))
        SY(lambda e: e.dma_start(out=a_p[:], in_=alog_h))
        SY(lambda e: e.dma_start(out=normw_p[:],
                                 in_=normw_h.rearrange("o (b p) -> o p b", p=128).squeeze(0),
                                 allow_slow_non_contiguous=True))
        bar()
        V(lambda e: e.tensor_tensor(out=mask01[:], in0=rowA[:, 128:384], in1=rowA[:, 384:640],
                                    op=ALU.is_ge))
        bar()
        V(lambda e: e.tensor_scalar(out=maskbig[:], in0=mask01[:], scalar1=-1.0, scalar2=1.0,
                                    op0=ALU.mult, op1=ALU.add))
        bar()
        V(lambda e: e.tensor_scalar(out=maskbig[:], in0=maskbig[:], scalar1=1e30, scalar2=None,
                                    op0=ALU.mult))
        V(lambda e: e.tensor_scalar(out=ident[:], in0=rowA[:, 0:128], scalar1=iot_p[:], scalar2=None,
                                    op0=ALU.is_equal))
        S(lambda e: e.activation(out=a_p[:], in_=a_p[:], func=ACT.Exp))
        bar()
        V(lambda e: e.tensor_scalar(out=a_p[:], in0=a_p[:], scalar1=-1.0, scalar2=None, op0=ALU.mult))
        bar()

        if upto == "p0":
            bar()
            emit_blocks()
            return nc
        G(lambda e: e.memset(asum1[:], 0.0))
        G(lambda e: e.memset(asum2[:], 0.0))
        bar()
        seq = [(w1t_h, k, HF, asum1) for k in range(8)] + \
              [(w2t_h, k, D, asum2) for k in range(32)]
        PFQ = (GDp, SYp, SDp)

        def p1_load(i):
            wh, k, wid, accv = seq[i]
            PFQ[i % 3](lambda e, wh=wh, k=k, wid=wid, i=i: e.dma_start(
                out=wbuf[i % 2][:, 0:wid], in_=wh[k * 128:(k + 1) * 128, :]))

        p1_load(0)
        for i in range(len(seq)):
            wh, k, wid, accv = seq[i]
            wait_pf()
            if i + 1 < len(seq):
                p1_load(i + 1)
            V(lambda e, i=i, wid=wid: e.reduce_sum(out=acc_t[:], in_=wbuf[i % 2][:, 0:wid],
                                                   axis=AX.X, apply_absolute_value=True))
            V(lambda e: e.reduce_sum(out=trash[:], in_=iot_p[:], axis=AX.X))
            V(lambda e: e.tensor_copy(out=trash[:], in_=iot_p[:]))
            V(lambda e, accv=accv: e.tensor_tensor(out=accv[:], in0=accv[:], in1=acc_t[:], op=ALU.add))
            bar()
        P(lambda e: e.matmul(PSA[0:1, 0:1], onesf[:], asum1[:], start=True, stop=True))
        bar()
        S(lambda e: e.copy(out=wsc[0:1, 0:1], in_=PSA[0:1, 0:1]))
        bar()
        P(lambda e: e.matmul(PSA[0:1, 0:1], onesf[:], asum2[:], start=True, stop=True))
        bar()
        S(lambda e: e.copy(out=wsc[0:1, 1:2], in_=PSA[0:1, 0:1]))
        bar()
        V(lambda e: e.tensor_scalar(out=wsc[0:1, 2:4], in0=wsc[0:1, 0:2], scalar1=1.0 / (HF * D),
                                    scalar2=EPS, op0=ALU.mult, op1=ALU.max))
        bar()
        V(lambda e: e.reciprocal(out=wsc[0:1, 0:2], in_=wsc[0:1, 2:4]))
        bar()
        if dev:
            SD(lambda e: e.dma_start(out=dram["dbg_sc"][0:1, 0:8], in_=wsc[0:1, 0:8]))
            SD(lambda e: e.dma_start(out=dram["dbg_rows"][2:3, 0:128].rearrange("o t -> t o"),
                                     in_=asum1[:]))
            SD(lambda e: e.dma_start(out=dram["dbg_rows"][3:4, 0:128].rearrange("o t -> t o"),
                                     in_=asum2[:]))
        for (dst, sl) in ((rw1_p, 0), (rw2_p, 1), (ws1_p, 2), (ws2_p, 3)):
            P(lambda e, sl=sl: e.matmul(PSA[:, 2560:2561], onesrow[:], wsc[0:1, sl:sl + 1],
                                        start=True, stop=True))
            bar()
            S(lambda e, dst=dst: e.copy(out=dst[:], in_=PSA[:, 2560:2561]))
            bar()
        bar()

        if upto == "p1":
            bar()
            emit_blocks()
            return nc
        SYp(lambda e: e.dma_start(out=x2b_p2[0][:], in_=x_h[0:128, :]))
        for tt in range(NT):
            x_l = x2b_p2[tt % 2]
            wait_pf()
            if tt + 1 < NT:
                SYp(lambda e, tt=tt: e.dma_start(out=x2b_p2[(tt + 1) % 2][:],
                                                 in_=x_h[(tt + 1) * 128:(tt + 2) * 128, :]))
            V(lambda e, x_l=x_l: e.reduce_sum(out=sx[:], in_=x_l[:], axis=AX.X))
            V(lambda e: e.reduce_sum(out=trash[:], in_=iot_p[:], axis=AX.X))
            V(lambda e, x_l=x_l: e.tensor_tensor(out=xsq[:], in0=x_l[:], in1=x_l[:], op=ALU.mult))
            V(lambda e: e.tensor_copy(out=trash[:], in_=iot_p[:]))
            V(lambda e: e.reduce_sum(out=sq2[:], in_=xsq[:], axis=AX.X))
            V(lambda e: e.reduce_sum(out=trash[:], in_=iot_p[:], axis=AX.X))
            V(lambda e: e.tensor_copy(out=trash[:], in_=iot_p[:]))
            bar()
            V(lambda e: e.tensor_scalar(out=mneg[:], in0=sx[:], scalar1=-1.0 / D, scalar2=None,
                                        op0=ALU.mult))
            V(lambda e: e.tensor_scalar(out=tv2[:], in0=sq2[:], scalar1=1.0 / D, scalar2=None,
                                        op0=ALU.mult))
            V(lambda e: e.tensor_tensor(out=tv1[:], in0=mneg[:], in1=mneg[:], op=ALU.mult))
            V(lambda e: e.tensor_copy(out=trash[:], in_=iot_p[:]))
            V(lambda e: e.tensor_tensor(out=tv2[:], in0=tv2[:], in1=tv1[:], op=ALU.subtract))
            V(lambda e: e.tensor_copy(out=trash[:], in_=iot_p[:]))
            V(lambda e: e.tensor_scalar(out=tv2[:], in0=tv2[:], scalar1=EPS, scalar2=None, op0=ALU.add))
            V(lambda e: e.tensor_copy(out=trash[:], in_=iot_p[:]))
            V(lambda e: e.reciprocal(out=tv1[:], in_=tv2[:]))
            bar()
            S(lambda e: e.activation(out=rstd[:], in_=tv1[:], func=ACT.Sqrt))
            bar()
            V(lambda e, x_l=x_l: e.tensor_scalar(out=xnC[:], in0=x_l[:], scalar1=mneg[:], scalar2=rstd[:],
                                        op0=ALU.add, op1=ALU.mult))
            V(lambda e: e.tensor_copy(out=trash[:], in_=iot_p[:]))
            V(lambda e: e.tensor_copy(out=xnbB[:], in_=xnC[:]))
            V(lambda e: e.reduce_max(out=tv1[:], in_=xnC[:], axis=AX.X, apply_absolute_value=True))
            V(lambda e: e.reduce_sum(out=trash[:], in_=iot_p[:], axis=AX.X))
            V(lambda e: e.tensor_copy(out=trash[:], in_=iot_p[:]))
            V(lambda e: e.tensor_scalar(out=tv1[:], in0=tv1[:], scalar1=EPS, scalar2=None, op0=ALU.max))
            V(lambda e: e.tensor_copy(out=trash[:], in_=iot_p[:]))
            V(lambda e, tt=tt: e.tensor_copy(out=xs_all[:, tt:tt + 1], in_=tv1[:]))
            V(lambda e, tt=tt: e.tensor_copy(out=xs_pad[:, tt:tt + 1], in_=tv1[:]))
            V(lambda e: e.reciprocal(out=tv2[:], in_=tv1[:]))
            V(lambda e: e.tensor_copy(out=trash[:], in_=iot_p[:]))
            V(lambda e: e.tensor_scalar(out=r_p[:], in0=tv2[:], scalar1=127.0, scalar2=None, op0=ALU.mult))
            V(lambda e: e.tensor_copy(out=trash[:], in_=iot_p[:]))
            V(lambda e: e.tensor_scalar(out=p3iD[:], in0=xnC[:], scalar1=r_p[:], scalar2=128.0,
                                        op0=ALU.mult, op1=ALU.add))
            V(lambda e: e.tensor_copy(out=trash[:], in_=iot_p[:]))
            V(lambda e: e.scalar_tensor_tensor(out=fprD[:], in0=xnC[:], scalar=r_p[:],
                                               in1=p3iD[:], op0=ALU.mult, op1=ALU.subtract))
            VD()
            bar()
            if dev:
                SD(lambda e, tt=tt: e.dma_start(out=dram["dbg_xn"][tt], in_=xnC[:]))
            for k in range(8):
                (SY if k % 2 == 0 else SD)(lambda e, k=k, tt=tt: e.dma_start_transpose(
                    out=xnT_c(k)[:, tt * 128:(tt + 1) * 128], in_=xnbB[:, k * 128:(k + 1) * 128]))
            S(lambda e: e.activation(out=sgnD[:], in_=fprD[:], func=ACT.Sign, bias=cst_h[:]))
            bar()
            V(lambda e: e.scalar_tensor_tensor(out=xnbB[:], in0=sgnD[:], scalar=0.0,
                                               in1=p3iD[:], op0=ALU.is_gt, op1=ALU.add))
            VD()
            bar()
            for k in range(8):
                (SY if k % 2 == 0 else SD)(lambda e, k=k: e.dma_start_transpose(
                    out=tstB2[:, k * 128:(k + 1) * 128], in_=xnbB[:, k * 128:(k + 1) * 128]))
            bar()
            SD(lambda e, tt=tt: e.dma_start(out=iq_sp[tt].rearrange("k d t -> d k t"),
                                            in_=tstB2[:].rearrange("d (k t) -> d k t", t=128)))
        bar()

        if upto == "ln":
            bar()
            emit_blocks()
            return nc

        OF_LIST = [(32, 128), (33, 128), (34, 1)] + [(16 + d, 128) for d in range(16)] \
                  + [(z, 128) for z in range(16)]

        def load_of(idx):
            of, M = OF_LIST[idx]
            par = idx % 2
            SY(lambda e, of=of, M=M, par=par: e.dma_start(
                out=wpstg[par][:, 0:8 * M].rearrange("p (k c) -> p k c", c=M),
                in_=wpt_h[:, of * 128:of * 128 + M].rearrange("(k p) c -> p k c", p=128),
                allow_slow_non_contiguous=(M < 128)))

        def proj_mms(idx):
            of, M = OF_LIST[idx]
            par = idx % 2
            V(lambda e, M=M, par=par: e.tensor_copy(out=wpa[par][:, 0:8 * M],
                                                    in_=wpstg[par][:, 0:8 * M]))
            bar()
            if idx + 1 < len(OF_LIST):
                load_of(idx + 1)
            for k in range(8):
                for tc in range(4):
                    P(lambda e, k=k, tc=tc, M=M, par=par: e.matmul(
                        PSA[0:M, tc * 512:(tc + 1) * 512],
                        wpa[par][:, k * M:(k + 1) * M],
                        xnT_c(k)[:, tc * 512:(tc + 1) * 512],
                        start=(k == 0), stop=(k == 7)))
            bar()

        load_of(0)
        bar()
        for idx in range(len(OF_LIST)):
            of, M = OF_LIST[idx]
            proj_mms(idx)
            if of == 32 or of == 33:
                S(lambda e: e.copy(out=tstage[:], in_=PSA[:, 0:T]))
                SDr()
                bar()
                SD(lambda e, i_sp=of - 32: e.dma_start(out=bc_sp[i_sp], in_=tstage[:]))
                bar()
            elif of == 34:
                S(lambda e: e.copy(out=rowA[0:1, :], in_=PSA[0:1, 0:T]))
                bar()
                G(lambda e: e.memset(PBF[:, 17664:28288], 0.0))
                bar()
            elif of >= 16:
                dblk = of - 16
                S(lambda e: e.activation(out=xmtile[:, 4:4 + T], in_=PSA[:, 0:T], func=ACT.Silu))
                bar()
                V(lambda e: e.tensor_scalar(out=xmtile[:, 4:4 + T], in0=xmtile[:, 4:4 + T],
                                            scalar1=8.0, scalar2=None, op0=ALU.min))
                bar()
                for kk in range(4):
                    V(lambda e, kk=kk, dblk=dblk: e.tensor_scalar(
                        out=xw[:, kk * (T + 4) + 4 + (3 - kk): kk * (T + 4) + 4 + (3 - kk) + T],
                        in0=xmtile[:, 4:4 + T],
                        scalar1=convw_t[:, dblk * 4 + kk: dblk * 4 + kk + 1],
                        scalar2=None, op0=ALU.mult))
                bar()
                for kk in range(4):
                    for tc in range(4):
                        base = kk * (T + 4) + 4
                        P(lambda e, kk=kk, tc=tc, base=base: e.matmul(
                            PSA[:, T + tc * 512: T + (tc + 1) * 512], ident[:],
                            xw[:, base + tc * 512: base + (tc + 1) * 512],
                            start=(kk == 0), stop=(kk == 3)))
                bar()
                S(lambda e: e.activation(out=xm2b[:], in_=PSA[:, T:2 * T], func=ACT.Silu))
                bar()
                V(lambda e: e.tensor_scalar(out=xm2b[:], in0=xm2b[:], scalar1=8.0, scalar2=None,
                                            op0=ALU.min))
                VD()
                bar()
                for cblk in range(16):
                    (SY if cblk % 2 == 0 else SD)(lambda e, cblk=cblk: e.dma_start_transpose(
                        out=tstage[:, cblk * 128:(cblk + 1) * 128],
                        in_=xm2b[:, cblk * 128:(cblk + 1) * 128]))
                bar()
                SD(lambda e, dblk=dblk: e.dma_start(
                    out=xm_sp[:, dblk].rearrange("c t d -> t c d"),
                    in_=tstage[:].rearrange("t (c d) -> t c d", d=128)))
            else:
                zblk = of
                S(lambda e: e.activation(out=xm2b[:], in_=PSA[:, 0:T], func=ACT.Silu))
                bar()
                V(lambda e: e.tensor_scalar(out=xm2b[:], in0=xm2b[:], scalar1=8.0, scalar2=None,
                                            op0=ALU.min))
                VD()
                bar()
                SD(lambda e, zblk=zblk: e.dma_start(
                    out=z_sp[:, zblk].rearrange("c z t -> z c t"),
                    in_=xm2b[:].rearrange("z (c t) -> z c t", t=128)))
        bar()

        if upto == "inproj":
            bar()
            emit_blocks()
            return nc

        for k in range(16):
            GD(lambda e, k=k: e.dma_start(out=bc128[:, 0:D], in_=wot_h[k * 128:(k + 1) * 128, :]))
            bar()
            V(lambda e, k=k: e.tensor_copy(out=wot_c(k), in_=bc128[:, 0:D]))
            bar()
        for k in range(16):
            V(lambda e, k=k: e.tensor_scalar(out=wot_c(k), in0=wot_c(k), scalar1=normw_p[:, k:k + 1],
                                             scalar2=None, op0=ALU.mult))
        bar()
        SY(lambda e: e.dma_start(out=bc128[:, 0:DI], in_=dp_h[0:1, :].squeeze(0).partition_broadcast(128)))
        bar()
        V(lambda e: e.tensor_copy(out=dp_bcb[:], in_=bc128[:, 0:DI]))
        bar()
        for j in range(4):
            P(lambda e, j=j: e.matmul(PSA[:, j * 512:(j + 1) * 512], onesrow[:],
                                      rowA[0:1, j * 512:(j + 1) * 512], start=True, stop=True))
        bar()
        S(lambda e: e.copy(out=rowB[:], in_=PSA[:, 0:T]))
        bar()
        S(lambda e: e.activation(out=rowB[:], in_=rowB[:], func=ACT.Exp))
        bar()
        S(lambda e: e.activation(out=rowB[:], in_=rowB[:], func=ACT.Ln, bias=1.0))
        G(lambda e: e.memset(rowA[:], 0.0))
        bar()
        V(lambda e: e.tensor_tensor_scan(out=s_bc[:], data0=rowB[:], data1=rowA[:],
                                         initial=0.0, op0=ALU.add, op1=ALU.add))
        bar()
        if dev:
            SD(lambda e: e.dma_start(out=dram["dbg_rows"][0:1], in_=rowB[0:1, :]))
            SD(lambda e: e.dma_start(out=dram["dbg_rows"][1:2], in_=s_bc[0:1, :]))
        GD(lambda e: e.dma_start(out=b_fb[:], in_=bc_sp[0]))
        GD(lambda e: e.dma_start(out=c_fb[:], in_=bc_sp[1]))
        bar()
        V(lambda e: e.tensor_tensor(out=bbar_b[:], in0=b_fb[:], in1=rowB[:], op=ALU.mult))
        bar()
        V(lambda e: e.tensor_copy(out=rowA[:, 0:1], in_=s_bc[:, 127:128]))
        V(lambda e: e.tensor_tensor(out=rowA[:, 1:NCH], in0=s_bc[:, 2 * L - 1::L],
                                    in1=s_bc[:, L - 1:T - L:L], op=ALU.subtract))
        bar()
        S(lambda e: e.activation(out=lam_c[:], in_=rowA[:, 0:NCH], func=ACT.Exp, scale=a_p[:]))
        bar()
        V(lambda e: e.tensor_tensor(
            out=rowA[:].rearrange("p (c t) -> p c t", t=L),
            in0=s_bc[:, L - 1::L].unsqueeze(2).broadcast_to([128, NCH, L]),
            in1=s_bc[:].rearrange("p (c t) -> p c t", t=L), op=ALU.subtract))
        bar()
        S(lambda e: e.activation(out=erow[:], in_=rowA[:], func=ACT.Exp, scale=a_p[:]))
        bar()
        V(lambda e: e.tensor_tensor(out=btil_b[:], in0=bbar_b[:], in1=erow[:], op=ALU.mult))
        bar()
        G(lambda e: e.memset(rowA[:, 0:L], 0.0))
        bar()
        V(lambda e: e.tensor_copy(
            out=rowA[:, L:T].rearrange("p (c t) -> p c t", t=L),
            in_=s_bc[:, L - 1:T - L:L].unsqueeze(2).broadcast_to([128, NCH - 1, L])))
        bar()
        V(lambda e: e.tensor_tensor(out=rowA[:], in0=s_bc[:], in1=rowA[:], op=ALU.subtract))
        bar()
        S(lambda e: e.activation(out=erow[:], in_=rowA[:], func=ACT.Exp, scale=a_p[:]))
        bar()
        V(lambda e: e.tensor_tensor(out=ctil_b[:], in0=c_fb[:], in1=erow[:], op=ALU.mult))
        bar()
        G(lambda e: e.memset(rowA[:, 0:SUB], 0.0))
        bar()
        V(lambda e: e.tensor_copy(
            out=rowA[:, SUB:T].rearrange("p (c t) -> p c t", t=SUB),
            in_=s_bc[:, SUB - 1:T - SUB:SUB].unsqueeze(2).broadcast_to([128, T // SUB - 1, SUB])))
        bar()
        V(lambda e: e.tensor_tensor(out=rowA[:], in0=s_bc[:], in1=rowA[:], op=ALU.subtract))
        bar()
        S(lambda e: e.activation(out=erow[:], in_=rowA[:], func=ACT.Exp, scale=a_p[:]))
        bar()
        V(lambda e: e.tensor_tensor(out=cloc_b[:], in0=c_fb[:], in1=erow[:], op=ALU.mult))
        VD()
        bar()
        if dev:
            SD(lambda e: e.dma_start(out=dram["dbg_bc"][0], in_=btil_b[:]))
            SD(lambda e: e.dma_start(out=dram["dbg_bc"][1], in_=ctil_b[:]))
            bar()

        if upto == "scanprep":
            bar()
            emit_blocks()
            return nc

        G(lambda e: e.memset(h_st[:], 0.0))
        bar()
        for c in range(NCH):
            V(lambda e, c=c: e.tensor_tensor(
                out=rowA[:].rearrange("p (b tp t) -> p b tp t", tp=SUB, t=SUB),
                in0=s_bc[:, c * L:(c + 1) * L].rearrange("p (b t) -> p b t", t=SUB)
                    .unsqueeze(2).broadcast_to([128, NSUB, SUB, SUB]),
                in1=s_bc[:, c * L:(c + 1) * L].rearrange("p (b tp) -> p b tp", tp=SUB)
                    .unsqueeze(3).broadcast_to([128, NSUB, SUB, SUB]),
                op=ALU.subtract))
            V(lambda e: e.tensor_tensor(
                out=rowA[:].rearrange("p (b q) -> p b q", q=SUB * SUB),
                in0=rowA[:].rearrange("p (b q) -> p b q", q=SUB * SUB),
                in1=maskbig[:].unsqueeze(1).broadcast_to([128, NSUB, SUB * SUB]), op=ALU.add))
            bar()
            S(lambda e: e.activation(out=epair[:], in_=rowA[:], func=ACT.Exp, scale=a_p[:]))
            V(lambda e, c=c: e.tensor_tensor(
                out=zpair[:].rearrange("s (b tp t) -> s b tp t", tp=SUB, t=SUB),
                in0=c_fb[:, c * L:(c + 1) * L].rearrange("s (b t) -> s b t", t=SUB)
                    .unsqueeze(2).broadcast_to([128, NSUB, SUB, SUB]),
                in1=bbar_b[:, c * L:(c + 1) * L].rearrange("s (b tp) -> s b tp", tp=SUB)
                    .unsqueeze(3).broadcast_to([128, NSUB, SUB, SUB]),
                op=ALU.mult))
            bar()
            V(lambda e: e.tensor_tensor(out=zpair[:], in0=zpair[:], in1=epair[:], op=ALU.mult))
            bar()
            for j in range(4):
                P(lambda e, j=j: e.matmul(PSA[0:1, j * 512:(j + 1) * 512], onesb[:],
                                          zpair[:, j * 512:(j + 1) * 512], start=True, stop=True))
            bar()
            S(lambda e: e.copy(out=rowB[0:1, :], in_=PSA[0:1, 0:T]))
            bar()
            V(lambda e: e.tensor_tensor(
                out=zpair[0:1, :].rearrange("o (b q) -> o b q", q=SUB * SUB),
                in0=rowB[0:1, :].rearrange("o (b q) -> o b q", q=SUB * SUB),
                in1=mask01[0:1, :].unsqueeze(1).broadcast_to([1, NSUB, SUB * SUB]), op=ALU.mult))
            bar()
            for b in range(NSUB):
                GD(lambda e, b=b: e.dma_start(
                    out=dram["m_sp"][b].rearrange("tp t -> (tp t)").unsqueeze(0),
                    in_=zpair[0:1, b * 256:(b + 1) * 256]))
            bar()
            for b in range(NSUB):
                GD(lambda e, b=b: e.dma_start(
                    out=mt_sb[b * SUB:(b + 1) * SUB, b * SUB:(b + 1) * SUB],
                    in_=dram["m_sp"][b]))
            bar()
            offs = [0, 0, 16, 48, 96, 160, 240, 336]
            for i in range(1, NSUB):
                w_ = SUB * i
                V(lambda e, c=c, i=i, w_=w_, o=offs[i]: e.tensor_scalar(
                    out=bc128[:, o:o + w_], in0=s_bc[:, c * L:c * L + w_],
                    scalar1=-1.0, scalar2=s_bc[:, c * L + i * SUB - 1:c * L + i * SUB],
                    op0=ALU.mult, op1=ALU.add))
            bar()
            S(lambda e: e.activation(out=erow[:, 0:448], in_=bc128[:, 0:448],
                                     func=ACT.Exp, scale=a_p[:]))
            bar()
            for i in range(1, NSUB):
                w_ = SUB * i
                V(lambda e, c=c, i=i, w_=w_, o=offs[i]: e.tensor_tensor(
                    out=epair[:, o:o + w_], in0=bbar_b[:, c * L:c * L + w_],
                    in1=erow[:, o:o + w_], op=ALU.mult))
            bar()
            for i in range(1, NSUB):
                w_ = SUB * i
                P(lambda e, c=c, i=i, w_=w_, o=offs[i]: e.matmul(
                    PSA[0:w_, 2048 + i * SUB:2048 + (i + 1) * SUB], epair[:, o:o + w_],
                    cloc_b[:, c * L + i * SUB:c * L + (i + 1) * SUB], start=True, stop=True))
            bar()
            for i in range(1, NSUB):
                w_ = SUB * i
                V(lambda e, i=i, w_=w_: e.tensor_copy(
                    out=mt_sb[0:w_, i * SUB:(i + 1) * SUB],
                    in_=PSA[0:w_, 2048 + i * SUB:2048 + (i + 1) * SUB]))
            bar()
            if dev:
                SD(lambda e, c=c: e.dma_start(out=dram["dbg_mt"][c], in_=mt_sb[:]))
            par = c % 2
            if c == 0:
                GDp(lambda e: e.dma_start(out=xmt2b[0][:].rearrange("t (b d) -> t b d", d=128),
                                          in_=xm_sp[0].rearrange("b t d -> t b d")))
                SYp(lambda e: e.dma_start_transpose(out=bT2b[0][:], in_=btil_b[:, 0:L]))
                SDp(lambda e: e.dma_start(out=zg2b[0][:].rearrange("z (b t) -> z b t", t=128),
                                          in_=z_sp[0].rearrange("b z t -> z b t")))
            wait_pf()
            xmt = xmt2b[par]
            zg_sl = zg2b[par]
            bT_t = bT2b[par]
            if c + 1 < NCH:
                GDp(lambda e, c=c, par=par: e.dma_start(
                    out=xmt2b[1 - par][:].rearrange("t (b d) -> t b d", d=128),
                    in_=xm_sp[c + 1].rearrange("b t d -> t b d")))
                SYp(lambda e, c=c, par=par: e.dma_start_transpose(
                    out=bT2b[1 - par][:], in_=btil_b[:, (c + 1) * L:(c + 2) * L]))
                SDp(lambda e, c=c, par=par: e.dma_start(
                    out=zg2b[1 - par][:].rearrange("z (b t) -> z b t", t=128),
                    in_=z_sp[c + 1].rearrange("b z t -> z b t")))
            bar()
            for tc in range(4):
                if c > 0:
                    P(lambda e, c=c, tc=tc: e.matmul(
                        PSA[:, tc * 512:(tc + 1) * 512], ctil_b[:, c * L:(c + 1) * L],
                        h_st[:, tc * 512:(tc + 1) * 512], start=True, stop=False))
                P(lambda e, c=c, tc=tc, xmt=xmt: e.matmul(
                    PSA[:, tc * 512:(tc + 1) * 512], mt_sb[:],
                    xmt[:, tc * 512:(tc + 1) * 512], start=(c == 0), stop=True))
            bar()
            V(lambda e, xmt=xmt: e.tensor_tensor(out=epair[:], in0=xmt[:], in1=dp_bcb[:], op=ALU.mult))
            V(lambda e: e.tensor_copy(out=trash[:], in_=iot_p[:]))
            V(lambda e: e.tensor_tensor(out=y_sbb[:], in0=epair[:], in1=PSA[:, 0:T], op=ALU.add))
            bar()
            if dev:
                SD(lambda e, c=c: e.dma_start(out=dram["dbg_y"][c], in_=y_sbb[:]))
            for tc in range(4):
                P(lambda e, tc=tc, xmt=xmt, bT_t=bT_t: e.matmul(
                    PSA[:, tc * 512:(tc + 1) * 512], bT_t[:],
                    xmt[:, tc * 512:(tc + 1) * 512], start=True, stop=True))
            bar()
            V(lambda e, c=c: e.scalar_tensor_tensor(out=h_st[:], in0=h_st[:], scalar=lam_c[:, c:c + 1],
                                                    in1=PSA[:, 0:T], op0=ALU.mult, op1=ALU.add))
            bar()
            if dev:
                SD(lambda e, c=c: e.dma_start(out=dram["dbg_h"][c], in_=h_st[:]))
            V(lambda e: e.tensor_tensor(out=ysq[:], in0=y_sbb[:], in1=y_sbb[:], op=ALU.mult))
            V(lambda e: e.tensor_copy(out=trash[:], in_=iot_p[:]))
            V(lambda e: e.reduce_sum(out=ssum[:], in_=ysq[:], axis=AX.X))
            V(lambda e: e.reduce_sum(out=trash[:], in_=iot_p[:], axis=AX.X))
            V(lambda e: e.tensor_copy(out=trash[:], in_=iot_p[:]))
            bar()
            V(lambda e: e.tensor_scalar(out=tv1[:], in0=ssum[:], scalar1=1.0 / DI, scalar2=EPS,
                                        op0=ALU.mult, op1=ALU.add))
            V(lambda e: e.tensor_copy(out=trash[:], in_=iot_p[:]))
            V(lambda e: e.reciprocal(out=tv2[:], in_=tv1[:]))
            bar()
            S(lambda e: e.activation(out=rstd[:], in_=tv2[:], func=ACT.Sqrt))
            bar()
            V(lambda e: e.tensor_scalar(out=y1b[:], in0=y_sbb[:], scalar1=rstd[:], scalar2=None,
                                        op0=ALU.mult))
            VD()
            bar()
            for b in range(16):
                (SY if b % 2 == 0 else SD)(lambda e, b=b: e.dma_start_transpose(
                    out=ygfm[:, b * 128:(b + 1) * 128], in_=y1b[:, b * 128:(b + 1) * 128]))
            bar()
            V(lambda e, zg_sl=zg_sl: e.tensor_tensor(out=ygfm[:], in0=ygfm[:], in1=zg_sl[:], op=ALU.mult))
            bar()
            for b in range(16):
                for oc in range(2):
                    P(lambda e, b=b, oc=oc: e.matmul(
                        PSA[:, 3072 + oc * 512:3072 + (oc + 1) * 512],
                        ygfm[:, b * 128:(b + 1) * 128],
                        wot_c(b)[:, oc * 512:(oc + 1) * 512], start=(b == 0), stop=(b == 15)))
            bar()
            S(lambda e: e.copy(out=attn_ev[:], in_=PSA[:, 3072:4096]))
            SDr()
            bar()
            SD(lambda e, c=c: e.dma_start(out=attn_sp[c * 128:(c + 1) * 128, :], in_=attn_ev[:]))
            bar()

        if upto == "scan":
            bar()
            emit_blocks()
            return nc

        PFQ[0](lambda e: e.dma_start(out=wbuf[0][:], in_=w1t_h[0:128, :]))
        for k in range(8):
            buf = wbuf[k % 2]
            wait_pf()
            if k + 1 < 8:
                PFQ[(k + 1) % 3](lambda e, k=k: e.dma_start(
                    out=wbuf[(k + 1) % 2][:], in_=w1t_h[(k + 1) * 128:(k + 2) * 128, :]))
            S(lambda e, buf=buf: e.activation(out=s1t[:], in_=buf[:], func=ACT.Sign,
                                              scale=rw1_p[:], bias=cst_m5[:]))
            S(lambda e, buf=buf: e.activation(out=s2t[:], in_=buf[:], func=ACT.Sign,
                                              scale=rw1_p[:], bias=cst_p5[:]))
            bar()
            V(lambda e, k=k: e.tensor_tensor(out=tq1_c(k), in0=s1t[:], in1=s2t[:], op=ALU.add))
            bar()
        if dev:
            SD(lambda e: e.dma_start(out=dram["dbg_tq1"], in_=tq1))
        stg1 = carve(PBF, 35328, T, BF16)
        stg2 = carve(PBF, 39424, T, BF16)
        for k in range(8):
            for j in range(8):
                P(lambda e, k=k, j=j: e.matmul(PSA[0:1, j * 512:(j + 1) * 512], onesb[:],
                                               tq1_c(k)[:, j * 512:(j + 1) * 512], start=True, stop=True))
            bar()
            S(lambda e: e.copy(out=rowA[0:1, :], in_=PSA[0:1, 0:T]))
            S(lambda e: e.copy(out=rowB[0:1, :], in_=PSA[0:1, T:2 * T]))
            bar()
            V(lambda e: e.tensor_copy(out=stg1[0:1, :], in_=rowA[0:1, :]))
            V(lambda e: e.tensor_copy(out=stg2[0:1, :], in_=rowB[0:1, :]))
            VD()
            bar()
            GD(lambda e, k=k: e.dma_start(out=rr1[1 + k:2 + k, 0:T], in_=stg1[0:1, :]))
            GD(lambda e, k=k: e.dma_start(out=rr1[1 + k:2 + k, T:HF], in_=stg2[0:1, :]))
            bar()
        SY(lambda e: e.dma_start(out=rowA[0:1, :], in_=b1_h[0:1, 0:T]))
        SY(lambda e: e.dma_start(out=rowB[0:1, :], in_=b1_h[0:1, T:HF]))
        bar()
        V(lambda e: e.tensor_copy(out=rr1[0:1, 0:T], in_=rowA[0:1, :]))
        V(lambda e: e.tensor_copy(out=rr1[0:1, T:HF], in_=rowB[0:1, :]))
        SY(lambda e: e.dma_start_transpose(out=xsT[:], in_=xs_pad[:]))
        bar()
        SD(lambda e: e.dma_start(out=dram["r_sp"], in_=xsT[0:16, :]))
        bar()
        SD(lambda e: e.dma_start(out=xsrow[0:1, :],
                                 in_=dram["r_sp"].rearrange("a t -> (a t)").unsqueeze(0)))
        bar()
        V(lambda e: e.tensor_scalar(out=rowA[0:1, :], in0=xsrow[0:1, :], scalar1=wsc[0:1, 2:3],
                                    scalar2=1.0 / 254.0, op0=ALU.mult, op1=ALU.mult))
        bar()
        V(lambda e: e.reciprocal(out=rowB[0:1, :], in_=rowA[0:1, :]))
        bar()
        G(lambda e: e.memset(rank1[0:9, :], -128.0))
        bar()
        V(lambda e: e.tensor_copy(out=rank1[0:1, :], in_=rowB[0:1, :]))
        bar()
        if dev:
            SD(lambda e: e.dma_start(out=dram["dbg_r1"], in_=rank1[0:9, :]))
            SD(lambda e: e.dma_start(out=dram["dbg_rr1"], in_=rr1[0:9, :]))
            bar()

        GD(lambda e: e.dma_start(out=iq_t[0][:].rearrange("d (k t) -> d k t", t=128),
                                 in_=iq_sp[0].rearrange("k d t -> d k t")))
        bar()
        for tt in range(NT):
            par = tt % 2
            if tt + 1 < NT:
                GD(lambda e, tt=tt, par=par: e.dma_start(
                    out=iq_t[1 - par][:].rearrange("d (k t) -> d k t", t=128),
                    in_=iq_sp[tt + 1].rearrange("k d t -> d k t")))
            for half in range(2):
                for k in range(8):
                    for j in range(4):
                        P(lambda e, half=half, k=k, j=j, par=par: e.matmul(
                            PSA[:, half * 2048 + j * 512: half * 2048 + (j + 1) * 512],
                            iq_t[par][:, k * 128:(k + 1) * 128],
                            tq1_c(k)[:, half * 2048 + j * 512: half * 2048 + (j + 1) * 512],
                            start=(k == 0), stop=False))
                for j in range(4):
                    P(lambda e, tt=tt, half=half, j=j: e.matmul(
                        PSA[:, half * 2048 + j * 512: half * 2048 + (j + 1) * 512],
                        rank1[0:9, tt * 128:(tt + 1) * 128],
                        rr1[0:9, half * 2048 + j * 512: half * 2048 + (j + 1) * 512],
                        start=False, stop=True))
            bar()
            V(lambda e, tt=tt: e.tensor_scalar(out=c1_p[:], in0=xs_all[:, tt:tt + 1],
                                               scalar1=ws1_p[:], scalar2=1.0 / 254.0,
                                               op0=ALU.mult, op1=ALU.mult))
            bar()
            for half in range(2):
                S(lambda e, half=half: e.activation(out=g_t[:, half * 2048:(half + 1) * 2048],
                                                    in_=PSA[:, half * 2048:(half + 1) * 2048],
                                                    func=ACT.Gelu, scale=c1_p[:]))
            bar()
            if dev:
                SD(lambda e, tt=tt: e.dma_start(out=dram["dbg_g"][tt], in_=g_t[:]))
            V(lambda e: e.reduce_max(out=tv1[:], in_=g_t[:], axis=AX.X, apply_absolute_value=True))
            V(lambda e: e.reduce_sum(out=trash[:], in_=iot_p[:], axis=AX.X))
            V(lambda e: e.tensor_copy(out=trash[:], in_=iot_p[:]))
            V(lambda e: e.tensor_scalar(out=tv1[:], in0=tv1[:], scalar1=EPS, scalar2=None, op0=ALU.max))
            V(lambda e: e.tensor_copy(out=trash[:], in_=iot_p[:]))
            V(lambda e, tt=tt: e.tensor_copy(out=hs_all[:, tt:tt + 1], in_=tv1[:]))
            V(lambda e, tt=tt: e.tensor_copy(out=xs_pad[:, 64 + tt:65 + tt], in_=tv1[:]))
            V(lambda e: e.reciprocal(out=tv2[:], in_=tv1[:]))
            V(lambda e: e.tensor_copy(out=trash[:], in_=iot_p[:]))
            V(lambda e: e.tensor_scalar(out=r_p[:], in0=tv2[:], scalar1=127.0, scalar2=None,
                                        op0=ALU.mult))
            V(lambda e: e.tensor_copy(out=trash[:], in_=iot_p[:]))
            V(lambda e: e.tensor_scalar(out=p3i[:], in0=g_t[:], scalar1=r_p[:], scalar2=128.0,
                                        op0=ALU.mult, op1=ALU.add))
            V(lambda e: e.tensor_copy(out=trash[:], in_=iot_p[:]))
            V(lambda e: e.scalar_tensor_tensor(out=fpr[:], in0=g_t[:], scalar=r_p[:],
                                               in1=p3i[:], op0=ALU.mult, op1=ALU.subtract))
            bar()
            S(lambda e: e.activation(out=sgn[:], in_=fpr[:], func=ACT.Sign, bias=cst_h[:]))
            bar()
            V(lambda e: e.scalar_tensor_tensor(out=gq_tm[:], in0=sgn[:], scalar=0.0,
                                               in1=p3i[:], op0=ALU.is_gt, op1=ALU.add))
            VD()
            bar()
            for jb in range(32):
                (SY if jb % 2 == 0 else SD)(lambda e, jb=jb: e.dma_start_transpose(
                    out=gq_fm[:, jb * 128:(jb + 1) * 128], in_=gq_tm[:, jb * 128:(jb + 1) * 128]))
            bar()
            SD(lambda e, tt=tt: e.dma_start(out=gq_sp[tt].rearrange("b j t -> j b t"),
                                            in_=gq_fm[:].rearrange("j (b t) -> j b t", t=128)))
            bar()

        if upto == "mm1":
            bar()
            emit_blocks()
            return nc

        PFQ[0](lambda e: e.dma_start(out=wbuf[0][:, 0:D], in_=w2t_h[0:128, :]))
        for k in range(32):
            buf = wbuf[k % 2]
            wait_pf()
            if k + 1 < 32:
                PFQ[(k + 1) % 3](lambda e, k=k: e.dma_start(
                    out=wbuf[(k + 1) % 2][:, 0:D], in_=w2t_h[(k + 1) * 128:(k + 2) * 128, :]))
            S(lambda e, buf=buf: e.activation(out=s1t[:, 0:D], in_=buf[:, 0:D], func=ACT.Sign,
                                              scale=rw2_p[:], bias=cst_m5[:]))
            S(lambda e, buf=buf: e.activation(out=s2t[:, 0:D], in_=buf[:, 0:D], func=ACT.Sign,
                                              scale=rw2_p[:], bias=cst_p5[:]))
            bar()
            V(lambda e, k=k: e.tensor_tensor(out=tq2_c(k), in0=s1t[:, 0:D], in1=s2t[:, 0:D], op=ALU.add))
            bar()
        if dev:
            SD(lambda e: e.dma_start(out=dram["dbg_tq2"], in_=tq2))
        stg1b = carve(PBF, 35328, 2 * T, BF16)
        for kg in range(8):
            for kk in range(4):
                for j in range(2):
                    P(lambda e, kg=kg, kk=kk, j=j: e.matmul(
                        PSA[0:1, kk * 1024 + j * 512: kk * 1024 + (j + 1) * 512], onesb[:],
                        tq2_c(kg * 4 + kk)[:, j * 512:(j + 1) * 512], start=True, stop=True))
            bar()
            S(lambda e: e.copy(out=rowA[0:1, :], in_=PSA[0:1, 0:T]))
            S(lambda e: e.copy(out=rowB[0:1, :], in_=PSA[0:1, T:2 * T]))
            bar()
            V(lambda e: e.tensor_copy(out=stg1b[0:1, 0:T], in_=rowA[0:1, :]))
            V(lambda e: e.tensor_copy(out=stg1b[0:1, T:2 * T], in_=rowB[0:1, :]))
            VD()
            bar()
            for kk in range(4):
                GD(lambda e, kg=kg, kk=kk: e.dma_start(out=rr2[1 + kg * 4 + kk:2 + kg * 4 + kk, 0:D],
                                                       in_=stg1b[0:1, kk * D:(kk + 1) * D]))
            bar()
        SY(lambda e: e.dma_start(out=rowA[0:1, 0:D], in_=b2_h))
        bar()
        V(lambda e: e.tensor_copy(out=rr2[0:1, 0:D], in_=rowA[0:1, 0:D]))
        SY(lambda e: e.dma_start_transpose(out=xsT[:], in_=xs_pad[:]))
        bar()
        SD(lambda e: e.dma_start(out=dram["r_sp"], in_=xsT[64:80, :]))
        bar()
        SD(lambda e: e.dma_start(out=hsrow[0:1, :],
                                 in_=dram["r_sp"].rearrange("a t -> (a t)").unsqueeze(0)))
        bar()
        V(lambda e: e.tensor_scalar(out=rowA[0:1, :], in0=hsrow[0:1, :], scalar1=wsc[0:1, 3:4],
                                    scalar2=1.0 / 254.0, op0=ALU.mult, op1=ALU.mult))
        bar()
        V(lambda e: e.reciprocal(out=rowB[0:1, :], in_=rowA[0:1, :]))
        bar()
        G(lambda e: e.memset(rank2[0:33, :], -128.0))
        bar()
        V(lambda e: e.tensor_copy(out=rank2[0:1, :], in_=rowB[0:1, :]))
        bar()

        def p9_load(tt):
            GDp(lambda e, tt=tt: e.dma_start(
                out=gq2b[tt % 2][:].rearrange("j (b t) -> j b t", t=128),
                in_=gq_sp[tt].rearrange("b j t -> j b t")))
            SDp(lambda e, tt=tt: e.dma_start(out=at2b[tt % 2][:], in_=attn_sp[tt * 128:(tt + 1) * 128, :]))
            SYp(lambda e, tt=tt: e.dma_start(out=xt2b[tt % 2][:], in_=x_h[tt * 128:(tt + 1) * 128, :]))

        p9_load(0)
        for tt in range(NT):
            gq_l = gq2b[tt % 2]
            at_l = at2b[tt % 2]
            xt_l = xt2b[tt % 2]
            wait_pf()
            if tt + 1 < NT:
                p9_load(tt + 1)
            V(lambda e, xt_l=xt_l, at_l=at_l: e.tensor_tensor(
                out=xa_t[:], in0=xt_l[:], in1=at_l[:], op=ALU.add))
            for k in range(32):
                for j in range(2):
                    P(lambda e, k=k, j=j, gq_l=gq_l: e.matmul(
                        PSA[:, j * 512:(j + 1) * 512], gq_l[:, k * 128:(k + 1) * 128],
                        tq2_c(k)[:, j * 512:(j + 1) * 512], start=(k == 0), stop=False))
            for j in range(2):
                P(lambda e, tt=tt, j=j: e.matmul(
                    PSA[:, j * 512:(j + 1) * 512], rank2[0:33, tt * 128:(tt + 1) * 128],
                    rr2[0:33, j * 512:(j + 1) * 512], start=False, stop=True))
            bar()
            V(lambda e, tt=tt: e.tensor_scalar(out=c2_p[:], in0=hs_all[:, tt:tt + 1],
                                               scalar1=ws2_p[:], scalar2=1.0 / 254.0,
                                               op0=ALU.mult, op1=ALU.mult))
            bar()
            V(lambda e: e.scalar_tensor_tensor(out=out_t[:], in0=PSA[:, 0:D], scalar=c2_p[:],
                                               in1=xa_t[:], op0=ALU.mult, op1=ALU.add))
            VD()
            bar()
            SY(lambda e, tt=tt: e.dma_start(out=out_h[tt * 128:(tt + 1) * 128, :], in_=out_t[:]))
            bar()

        bar()
        emit_blocks()
        return nc


def host_prep(inputs):
    f = np.ascontiguousarray
    wpt = f(inputs["L0_attn_in_proj_w"].T)
    convw = f(inputs["L0_attn_conv_w"].reshape(DI, 4))
    alog = f(inputs["L0_attn_A_log"].reshape(DS, 1))
    dp = f(inputs["L0_attn_Dp"].reshape(1, DI))
    normw = f(inputs["L0_attn_norm_w"].reshape(1, DI))
    wot = f(inputs["L0_attn_out_proj_w"].T)
    w1t = f(inputs["L0_ffn_W1"].T)
    b1 = f(inputs["L0_ffn_b1"].reshape(1, HF))
    w2t = f(inputs["L0_ffn_W2"].T)
    b2 = f(inputs["L0_ffn_b2"].reshape(1, D))
    x = inputs["L0_attn_x"]
    in_maps = []
    for c in range(8):
        in_maps.append({
            "x": f(x[c]), "wpt": wpt, "convw": convw, "alog": alog, "dp": dp,
            "normw": normw, "wot": wot, "w1t": w1t, "b1": b1, "w2t": w2t, "b2": b2,
        })
    return in_maps


def kernel(**inputs):
    in_maps = host_prep(inputs)
    nc = bass.Bass("TRN2", target_bir_lowering=False, debug=False, detect_race_conditions=False)
    build_core(nc, dev=False, upto="all")
    res = run_bass_kernel_spmd(nc, in_maps, core_ids=list(range(8)))
    out = np.stack([np.asarray(r["out"]) for r in res.results], axis=0)
    return out.astype(np.float32)
